# version 12 (final)
# baseline (speedup 1.0000x reference)
import numpy as np
import concourse.bass as bass
import concourse.mybir as mybir
from concourse.bass_utils import run_bass_kernel_spmd
from contextlib import ExitStack

F32 = mybir.dt.float32
BF16 = mybir.dt.bfloat16
I32 = mybir.dt.int32
AF = mybir.ActivationFunctionType
ALU = mybir.AluOpType
AX = mybir.AxisListType

NK = 16
NH = 44
HHALF = 22
NDO = 16
TQ = 1024
TG = 2048
NQT = 16
EPS = 1e-5

_TAPS = []


class Ch:
    def __init__(self, nc, es, name):
        self.h = es.enter_context(nc.semaphore(name))
        self.n = 0

    def inc(self, k=1):
        self.n += k
        return (self.h, k)

    def at(self, v=None):
        v = self.n if v is None else v
        assert v <= self.n, "wait on future inc -> deadlock"
        return (self.h, v)


class Prog:
    def __init__(self):
        self.ops = {k: [] for k in ("tensor", "vector", "scalar", "sync",
                                    "gpsimd")}

    def emit(self, eng, meth, *args, waits=(), incs=(), **kwargs):
        self.ops[eng].append((meth, args, kwargs, list(waits), list(incs)))


def _run_engine(eng, oplist):
    for meth, args, kwargs, waits, incs in oplist:
        for h, v in waits:
            eng.wait_ge(h, v)
        if meth is None:
            continue
        ins = getattr(eng, meth)(*args, **kwargs)
        for h, k in incs:
            ins.then_inc(h, k)


def build_nc(taps=(), stage="full", detect_races=True):
    nc = bass.Bass("TRN2", target_bir_lowering=False,
                   detect_race_conditions=detect_races)
    P = Prog()
    es = ExitStack()

    d_x = nc.dram_tensor("x_fm", [2048, TQ], F32, kind="ExternalInput")
    d_wgu = nc.dram_tensor("wgu_t", [NH, 128, 4096], I32, kind="ExternalInput")
    d_wd = nc.dram_tensor("wd_t", [NDO, 128, 5632], I32, kind="ExternalInput")
    d_waq = nc.dram_tensor("waq_t", [8, 128, 2048], F32, kind="ExternalInput")
    d_wak = nc.dram_tensor("wak_t", [2, 128, 2048], F32, kind="ExternalInput")
    d_wav = nc.dram_tensor("wav_t", [2, 128, 2048], F32, kind="ExternalInput")
    d_wao = nc.dram_tensor("wao_t", [16, 128, 2048], F32, kind="ExternalInput")
    d_cst = nc.dram_tensor("cst", [128, 640], F32, kind="ExternalInput")
    d_crow = nc.dram_tensor("cstrow", [1, 256], F32, kind="ExternalInput")
    d_tbl = nc.dram_tensor("tbl", [128, 4096], F32, kind="ExternalInput")
    d_y = nc.dram_tensor("y_fm", [2048, TQ], F32, kind="ExternalOutput")

    d_xpark = nc.dram_tensor("xpark", [128, NK * TQ], F32, kind="Internal")
    d_x1in = nc.dram_tensor("xch1_in", [2048, TQ], BF16, kind="Internal")
    d_x1out_a = nc.dram_tensor("xch1_out_a", [2048, TQ], BF16,
                               kind="Internal")
    d_x1out_b = nc.dram_tensor("xch1_out_b", [2048, TQ], BF16,
                               kind="Internal")
    d_x2in = nc.dram_tensor("xch2_in", [1024, TG], BF16, kind="Internal")
    d_x2out_a = nc.dram_tensor("xch2_out_a", [1024, TG], BF16,
                               kind="Internal")
    d_x2out_b = nc.dram_tensor("xch2_out_b", [1024, TG], BF16,
                               kind="Internal")

    C_ID, C_MF, C_MT, C_SG, C_SU, C_SD, C_ONE, C_S0, C_S1 = (
        0, 128, 256, 384, 428, 472, 488, 489, 490)
    C_EPS, C_NHALF, C_NEG1 = 491, 492, 493

    s_cst = es.enter_context(nc.sbuf_tensor("s_cst", [128, 640], F32))
    s_cbf = es.enter_context(nc.sbuf_tensor("s_cbf", [128, 520], BF16))
    s_crow = es.enter_context(nc.sbuf_tensor("s_crow", [1, 256], F32))
    RA = es.enter_context(nc.sbuf_tensor("rows_a", [128, 2 * TQ], F32))
    s_nm = es.enter_context(nc.sbuf_tensor("s_negmax", [128, 1], F32))
    s_rs = es.enter_context(nc.sbuf_tensor("s_rowsum", [128, 1], F32))
    s_lnr = es.enter_context(nc.sbuf_tensor("s_lnr", [128, 1], F32))
    s_nmla = es.enter_context(nc.sbuf_tensor("s_nmls_all", [128, 16], F32))
    s_nm2 = es.enter_context(nc.sbuf_tensor("s_negmax2", [128, 1], F32))
    s_rs2 = es.enter_context(nc.sbuf_tensor("s_rowsum2", [128, 1], F32))
    s_nmm = es.enter_context(nc.sbuf_tensor("s_nmm", [128, 1], F32))
    s_d1 = es.enter_context(nc.sbuf_tensor("s_d1", [128, 1], F32))
    s_d2 = es.enter_context(nc.sbuf_tensor("s_d2", [128, 1], F32))
    s_e1 = es.enter_context(nc.sbuf_tensor("s_e1", [128, 1], F32))
    s_e2 = es.enter_context(nc.sbuf_tensor("s_e2", [128, 1], F32))
    s_dbg = es.enter_context(nc.sbuf_tensor("s_dbg", [128, 4], F32))

    bufs = {}
    ch = {}
    for name in ("const", "cbf", "xload", "sq", "lnst", "lnrow", "lnbc", "xn",
                 "wgu_ld", "wgu_dn", "wgu_c", "wd_c", "ffn_mm", "silu", "prod", "cs_mm", "cs",
                 "ps_mm", "ps", "wd_ld", "wd_dn", "dn_mm", "dn_ev",
                 "park1", "pk1", "cc1", "x2f", "pw", "kmm", "ropek", "vmm",
                 "vev", "qmm", "ropeq", "tblld", "negmax",
                 "expS", "nmtr", "nmev", "expT", "pv", "otev", "otdma", "cc2",
                 "seg", "smm", "mg1", "mg2",
                 "lohi", "blend", "waob", "opj", "opj_ev", "park2",
                 "xrel", "xrel2", "out", "dbg"):
        ch[name] = Ch(nc, es, "c_" + name)

    def tap(tag, waits):
        for t, bname in taps:
            if t != tag:
                continue
            buf = bufs[bname]
            o = nc.dram_tensor(f"dbg_{tag}_{bname}",
                               [buf.shape[0], buf.shape[1]], buf.dtype,
                               kind="ExternalOutput")
            P.emit("sync", "dma_start", out=o[:, :], in_=buf[:, :],
                   waits=list(waits), incs=[ch["dbg"].inc(16)])

    def _build(final_waits):
        P.emit("sync", None, waits=list(final_waits) + [ch["dbg"].at()])
        with nc.Block() as block:
            @block.tensor
            def _(eng):
                _run_engine(eng, P.ops["tensor"])

            @block.vector
            def _(eng):
                _run_engine(eng, P.ops["vector"])

            @block.scalar
            def _(eng):
                _run_engine(eng, P.ops["scalar"])

            @block.sync
            def _(eng):
                _run_engine(eng, P.ops["sync"])

            @block.gpsimd
            def _(eng):
                _run_engine(eng, P.ops["gpsimd"])
        return nc

    def emit_ln(X, XN, XSQ, SG, PPa, PPb, chunk_wait, k0_extra=(),
                bcast_extra=()):
        onecol = s_cst[:, C_ONE:C_ONE + 1]
        onerow = s_crow[0:1, 128:256]
        r_sx, r_sxx = RA[0:1, 0:TQ], RA[0:1, TQ:2 * TQ]
        for k in range(NK):
            w = list(chunk_wait(k)) + (list(k0_extra) if k == 0 else [])
            if k >= 2:
                w.append((ch["lnst"].h, ch["lnst"].n - 1))
            P.emit("scalar", "square", XSQ[k % 2][:, :],
                   X[:, k * TQ:(k + 1) * TQ], waits=w, incs=[ch["sq"].inc()])
            wpe = list(chunk_wait(k)) + (list(k0_extra) if k == 0 else []) \
                + [ch["sq"].at()]
            for tc in range(2):
                P.emit("tensor", "matmul",
                       PPb[0:1, TQ + tc * 512:TQ + (tc + 1) * 512], onecol,
                       X[:, k * TQ + tc * 512:k * TQ + (tc + 1) * 512],
                       start=(k == 0), stop=(k == NK - 1),
                       waits=wpe if tc == 0 else [])
                P.emit("tensor", "matmul",
                       PPa[0:1, TQ + tc * 512:TQ + (tc + 1) * 512], onecol,
                       XSQ[k % 2][:, tc * 512:(tc + 1) * 512],
                       start=(k == 0), stop=(k == NK - 1),
                       incs=[ch["lnst"].inc()] if tc == 1 else [])
        P.emit("vector", "tensor_copy", r_sx, PPb[0:1, TQ:2 * TQ],
               waits=[ch["lnst"].at()])
        P.emit("vector", "tensor_copy", r_sxx, PPa[0:1, TQ:2 * TQ],
               incs=[ch["lnrow"].inc()])
        for tc in range(2):
            P.emit("tensor", "matmul", PPa[:, tc * 512:(tc + 1) * 512],
                   onerow, RA[0:1, tc * 512:(tc + 1) * 512],
                   start=True, stop=True,
                   waits=([ch["lnrow"].at()] + list(bcast_extra))
                   if tc == 0 else [])
            P.emit("tensor", "matmul",
                   PPa[:, TQ + tc * 512:TQ + (tc + 1) * 512],
                   onerow, RA[0:1, TQ + tc * 512:TQ + (tc + 1) * 512],
                   start=True, stop=True,
                   incs=[ch["lnbc"].inc()] if tc == 1 else [])
        P.emit("vector", "tensor_scalar_mul", XSQ[0][:, :], PPa[:, 0:TQ],
               1.0 / 2048.0, waits=[ch["lnbc"].at()])
        P.emit("vector", "scalar_tensor_tensor", XSQ[1][:, :], XSQ[0][:, :],
               -1.0, XSQ[0][:, :], op0=ALU.mult, op1=ALU.mult)
        P.emit("vector", "scalar_tensor_tensor", XSQ[1][:, :],
               PPa[:, TQ:2 * TQ], 1.0 / 2048.0, XSQ[1][:, :],
               op0=ALU.mult, op1=ALU.add, incs=[ch["lnrow"].inc()])
        P.emit("scalar", "activation", PPa[:, TQ:2 * TQ], XSQ[1][:, :],
               AF.Ln, bias=s_cst[:, C_EPS:C_EPS + 1],
               waits=[ch["lnrow"].at()])
        P.emit("scalar", "activation", XSQ[1][:, :], PPa[:, TQ:2 * TQ],
               AF.Exp, scale=s_cst[:, C_NHALF:C_NHALF + 1],
               incs=[ch["lnrow"].inc()])
        P.emit("vector", "scalar_tensor_tensor", SG[0][:, :], XSQ[0][:, :],
               -1.0, XSQ[1][:, :], op0=ALU.mult, op1=ALU.mult,
               waits=[ch["lnrow"].at()], incs=[ch["lnrow"].inc()])
        for k in range(NK):
            w = [ch["lnrow"].at()] if k == 0 else []
            P.emit("vector", "tensor_mul", SG[1][:, :],
                   X[:, k * TQ:(k + 1) * TQ], XSQ[1][:, :], waits=w)
            P.emit("vector", "tensor_add", XN[:, k * TQ:(k + 1) * TQ],
                   SG[1][:, :], SG[0][:, :],
                   incs=[ch["xn"].inc()])

    def emit_ffn(X, XN, PROD, WGU, WD, SG, PP, pass_idx, pp1_waits=(),
                 dma_first_waits=(), prologue_done=False):
        xn_ready = ch["xn"].at()
        xn_base = ch["xn"].n - NK

        def wgu_dma(hc):
            w = []
            if pass_idx == 0 and hc == 0:
                w = list(dma_first_waits)
            elif pass_idx == 1 and hc == 0:
                w = list(dma_first_waits)
            if pass_idx * NH + hc >= 3:
                w.append((ch["ffn_mm"].h, pass_idx * NH + hc - 2))
            P.emit("gpsimd", "dma_start", out=WGU[hc % 3][:, :],
                   in_=d_wgu[hc, :, :], waits=w, incs=[ch["wgu_ld"].inc(16)])

        def wgu_fix(hc):
            P.emit("vector", "tensor_scalar_add", WGU[hc % 3][:, :],
                   WGU[hc % 3][:, :], -128.0,
                   waits=[(ch["wgu_ld"].h, 16 * (pass_idx * NH + hc + 1))],
                   incs=[ch["wgu_c"].inc()])

        def wd_dma(do, half):
            idx = pass_idx * 32 + half * NDO + do
            w = list(dma_first_waits) if idx == 0 else []
            if idx >= 2:
                w.append((ch["dn_mm"].h, idx - 1))
            P.emit("gpsimd", "dma_start", out=WD[do % 2][:, :],
                   in_=d_wd[do, :, half * 2816:(half + 1) * 2816],
                   waits=w, incs=[ch["wd_ld"].inc(16)])

        def wd_fix(do, half):
            idx = pass_idx * 32 + half * NDO + do
            P.emit("vector", "tensor_scalar_add", WD[do % 2][:, :],
                   WD[do % 2][:, :], -128.0,
                   waits=[(ch["wd_ld"].h, 16 * (idx + 1))],
                   incs=[ch["wd_c"].inc()])

        for half in range(2):
            base = half * HHALF
            if not (half == 0 and prologue_done):
                for hc in range(base, base + 3):
                    wgu_dma(hc)
                wgu_fix(base)
                wgu_fix(base + 1)
            for i, hc in enumerate(range(base, base + HHALF)):
                p = hc % 3
                w = [(ch["wgu_c"].h, pass_idx * NH + hc + 1)]
                if hc < 2:
                    if hc == 1:
                        w += list(pp1_waits)
                else:
                    w.append((ch["silu"].h, pass_idx * NH + hc - 1))
                    w.append((ch["prod"].h, pass_idx * NH + hc - 1))
                if half == 1 and i < 2:
                    w.append((ch["dn_ev"].h, pass_idx * 32 + 16))
                first = True
                for mat in range(2):
                    for tc in range(2):
                        out = PP[hc % 2][:, mat * TQ + tc * 512:
                                         mat * TQ + (tc + 1) * 512]
                        for k in range(NK):
                            wk = list(w) if first else []
                            if hc < 2 and mat == 0 and tc == 0:
                                wk.append((ch["xn"].h, xn_base + k + 1))
                            P.emit("tensor", "matmul", out,
                                   WGU[p][:, mat * 2048 + k * 128:
                                          mat * 2048 + (k + 1) * 128],
                                   XN[:, k * TQ + tc * 512:
                                      k * TQ + (tc + 1) * 512],
                                   start=(k == 0), stop=(k == NK - 1),
                                   waits=wk,
                                   incs=([ch["ffn_mm"].inc()]
                                         if (mat == 1 and tc == 1
                                             and k == NK - 1) else []))
                            first = False
                w = [ch["ffn_mm"].at()]
                if hc >= 2:
                    w.append((ch["prod"].h, pass_idx * NH + hc - 1))
                P.emit("scalar", "activation", SG[hc % 2][:, :],
                       PP[hc % 2][:, 0:TQ], AF.Silu,
                       scale=s_cst[:, C_SG + hc:C_SG + hc + 1],
                       waits=w, incs=[ch["silu"].inc()])
                P.emit("vector", "scalar_tensor_tensor",
                       PROD[:, i * TQ:(i + 1) * TQ],
                       PP[hc % 2][:, TQ:2 * TQ],
                       s_cst[:, C_SU + hc:C_SU + hc + 1], SG[hc % 2][:, :],
                       op0=ALU.mult, op1=ALU.mult,
                       waits=[ch["silu"].at()], incs=[ch["prod"].inc()])
                if hc + 3 < base + HHALF:
                    wgu_dma(hc + 3)
                if hc + 2 < base + HHALF:
                    wgu_fix(hc + 2)
            prod_full = (ch["prod"].h, ch["prod"].n)
            for do in range(2):
                wd_dma(do, half)
            wd_fix(0, half)
            wd_fix(1, half)
            for do in range(NDO):
                idx = pass_idx * 32 + half * NDO + do
                p3 = do % 2
                w = [(ch["wd_c"].h, idx + 1), prod_full]
                if do >= 2:
                    w.append((ch["dn_ev"].h, idx - 1))
                first = True
                for tc in range(2):
                    out = PP[do % 2][:, tc * 512:(tc + 1) * 512]
                    for i in range(HHALF):
                        P.emit("tensor", "matmul", out,
                               WD[p3][:, i * 128:(i + 1) * 128],
                               PROD[:, i * TQ + tc * 512:
                                    i * TQ + (tc + 1) * 512],
                               start=(i == 0), stop=(i == HHALF - 1),
                               waits=w if first else [],
                               incs=([ch["dn_mm"].inc()]
                                     if (tc == 1 and i == HHALF - 1) else []))
                        first = False
                P.emit("vector", "scalar_tensor_tensor",
                       X[:, do * TQ:(do + 1) * TQ], PP[do % 2][:, 0:TQ],
                       s_cst[:, C_SD + do:C_SD + do + 1],
                       X[:, do * TQ:(do + 1) * TQ],
                       op0=ALU.mult, op1=ALU.add,
                       waits=[ch["dn_mm"].at()], incs=[ch["dn_ev"].inc()])
                if do + 2 < NDO:
                    wd_dma(do + 2, half)
                    wd_fix(do + 2, half)

    def ffn_prefetch(WGU, pass_idx, dma_first_waits=()):
        for hc in range(3):
            w = list(dma_first_waits) if hc == 0 else []
            if pass_idx * NH + hc >= 3:
                w.append((ch["ffn_mm"].h, pass_idx * NH + hc - 2))
            P.emit("gpsimd", "dma_start", out=WGU[hc % 3][:, :],
                   in_=d_wgu[hc, :, :], waits=w, incs=[ch["wgu_ld"].inc(16)])
        for hc in range(2):
            P.emit("vector", "tensor_scalar_add", WGU[hc % 3][:, :],
                   WGU[hc % 3][:, :], -128.0,
                   waits=[(ch["wgu_ld"].h, 16 * (pass_idx * NH + hc + 1))],
                   incs=[ch["wgu_c"].inc()])

    def emit_rope(PB, dst, toff, SCR1, SCR2, wait0, incs_last):
        tcc = bufs["Tcc"][:, toff:toff + 512]
        tss = bufs["Tss"][:, toff:toff + 512]
        P.emit("vector", "tensor_scalar_mul", SCR2[0:64, :], PB[64:128, :],
               -1.0, waits=list(wait0))
        P.emit("vector", "tensor_copy", SCR2[64:128, :], PB[0:64, :])
        P.emit("vector", "tensor_mul", SCR1[:, :], PB[:, :], tcc)
        P.emit("vector", "tensor_mul", SCR2[:, :], SCR2[:, :], tss)
        P.emit("vector", "tensor_add", dst, SCR1[:, :], SCR2[:, :],
               incs=list(incs_last))

    P.emit("sync", "dma_start", out=s_cst[:, :], in_=d_cst[:, :],
           incs=[ch["const"].inc(16)])
    P.emit("sync", "dma_start", out=s_crow[:, :], in_=d_crow[:, :],
           incs=[ch["const"].inc(16)])
    P.emit("gpsimd", "dma_start", out=s_cbf[:, 0:384], in_=d_cst[:, 0:384],
           incs=[ch["cbf"].inc(16)])
    P.emit("gpsimd", "dma_start", out=s_cbf[:, 384:385],
           in_=d_cst[:, C_ONE:C_ONE + 1], allow_slow_non_contiguous=True,
           incs=[ch["cbf"].inc(16)])
    P.emit("gpsimd", "dma_start", out=s_cbf[:, 392:520],
           in_=d_cst[:, 512:640], incs=[ch["cbf"].inc(16)])
    P.emit("vector", "tensor_scalar_mul", s_cst[:, C_SD:C_SD + NDO],
           s_cst[:, C_SD:C_SD + NDO], 0.5,
           waits=[ch["const"].at()], incs=[ch["const"].inc()])

    esA = ExitStack()
    X = esA.enter_context(nc.sbuf_tensor("X", [128, NK * TQ], F32))
    XN = esA.enter_context(nc.sbuf_tensor("XN", [128, NK * TQ], BF16))
    PROD = esA.enter_context(nc.sbuf_tensor("PROD", [128, HHALF * TQ], BF16))
    WGU = [esA.enter_context(nc.sbuf_tensor(f"WGU{i}", [128, 4096], BF16))
           for i in range(3)]
    WD = [esA.enter_context(nc.sbuf_tensor(f"WD{i}", [128, 2816], BF16))
          for i in range(2)]
    XSQ = [esA.enter_context(nc.sbuf_tensor(f"XSQ{i}", [128, TQ], F32))
           for i in range(2)]
    SG = [esA.enter_context(nc.sbuf_tensor(f"SGb{i}", [128, TQ], F32))
          for i in range(2)]
    psA = ExitStack()
    PP = [psA.enter_context(nc.psum_tensor(f"PP{i}", [128, 2048], F32))
          for i in range(2)]
    bufs.update(X=X, XN=XN, PROD=PROD)

    for k in range(NK):
        P.emit("sync", "dma_start",
               out=X[:, k * TQ:(k + 1) * TQ],
               in_=d_x.ap().rearrange("(k p) t -> p k t", p=128)[:, k, :],
               incs=[ch["xload"].inc(16)])

    ffn_prefetch(WGU, 0)
    emit_ln(X, XN, XSQ, SG, PP[0], PP[1],
            lambda k: [(ch["xload"].h, 16 * (k + 1))],
            k0_extra=[ch["const"].at()],
            bcast_extra=[ch["xload"].at()])
    tap("L1", [ch["xn"].at()])
    if stage == "L1":
        return _build([ch["xn"].at()])
    emit_ffn(X, XN, PROD, WGU, WD, SG, PP, 0, prologue_done=True)
    tap("F1", [ch["dn_ev"].at()])
    if stage == "F1":
        return _build([ch["dn_ev"].at()])
    emit_ln(X, XN, XSQ, SG, PP[0], PP[1],
            lambda k: [(ch["dn_ev"].h, 16 + k + 1)],
            k0_extra=[ch["prod"].at(), ch["silu"].at()],
            bcast_extra=[(ch["dn_ev"].h, 32)])
    tap("L2", [ch["xn"].at()])
    if stage == "L2":
        return _build([ch["xn"].at()])

    xn_b = ch["xn"].n - NK
    for half in range(2):
        P.emit("sync", "dma_start",
               out=d_x1in.ap().rearrange("(h k p) t -> p h k t", p=128, h=2)
               [:, half, :, :],
               in_=XN[:, half * 8 * TQ:(half + 1) * 8 * TQ]
               .rearrange("p (k t) -> p k t", k=8),
               waits=[(ch["xn"].h, xn_b + 8 * (half + 1))],
               incs=[ch["pk1"].inc(16)])
    P.emit("sync", "dma_start", out=d_xpark[:, :], in_=X[:, :],
           incs=[ch["park1"].inc(16)])
    P.emit("gpsimd", "collective_compute", "AllGather", ALU.bypass,
           replica_groups=[[0, 1], [2, 3], [4, 5], [6, 7]],
           ins=[d_x1in[0:1024, :].opt()], outs=[d_x1out_a.ap().opt()],
           waits=[(ch["pk1"].h, 16)], incs=[ch["cc1"].inc()])
    P.emit("gpsimd", "collective_compute", "AllGather", ALU.bypass,
           replica_groups=[[0, 1], [2, 3], [4, 5], [6, 7]],
           ins=[d_x1in[1024:2048, :].opt()], outs=[d_x1out_b.ap().opt()],
           waits=[(ch["pk1"].h, 32), (ch["cc1"].h, 1)],
           incs=[ch["cc1"].inc()])

    psA.close()
    esA.close()

    esB = ExitStack()
    K_f = esB.enter_context(nc.sbuf_tensor("K_f", [128, 2 * TG], BF16))
    VT_f = esB.enter_context(nc.sbuf_tensor("VT_f", [128, NQT * 2 * 128],
                                            BF16))
    Q = esB.enter_context(nc.sbuf_tensor("Q", [128, 8 * TG], BF16))
    esB2 = ExitStack()
    X2F = esB2.enter_context(nc.sbuf_tensor("X2F", [128, NK * TG], BF16))
    Tcc = esB2.enter_context(nc.sbuf_tensor("Tcc", [128, TG], F32))
    Tss = esB2.enter_context(nc.sbuf_tensor("Tss", [128, TG], F32))
    PW = esB2.enter_context(nc.sbuf_tensor("PW", [128, 4 * 2048], BF16))
    SCR1 = esB2.enter_context(nc.sbuf_tensor("SCR1", [128, 512], F32))
    SCR2 = esB2.enter_context(nc.sbuf_tensor("SCR2", [128, 512], F32))
    psB = ExitStack()
    PBA = psB.enter_context(nc.psum_tensor("PBA", [128, 512], F32))
    PBB = psB.enter_context(nc.psum_tensor("PBB", [128, 512], F32))
    PBC = psB.enter_context(nc.psum_tensor("PBC", [128, 256], F32))
    PBD = psB.enter_context(nc.psum_tensor("PBD", [128, 256], F32))
    bufs.update(K_f=K_f, VT_f=VT_f, Q=Q, X2F=X2F, Tcc=Tcc, Tss=Tss)

    sbr = [ch["park1"].at(), ch["pk1"].at()]
    P.emit("sync", "dma_start", out=Tcc[:, :], in_=d_tbl[:, 0:TG],
           waits=list(sbr), incs=[ch["tblld"].inc(16)])
    P.emit("sync", "dma_start", out=Tss[:, :], in_=d_tbl[:, TG:2 * TG],
           incs=[ch["tblld"].inc(16)])
    for half in range(2):
        srcH = (d_x1out_a, d_x1out_b)[half]
        for r in range(2):
            P.emit("sync", "dma_start",
                   out=X2F[:, half * 8 * TG:(half + 1) * 8 * TG]
                   .rearrange("p (k t) -> p k t", k=8)
                   [:, :, r * TQ:(r + 1) * TQ],
                   in_=srcH.ap().rearrange("(r k p) t -> p r k t",
                                           p=128, r=2)[:, r, :, :],
                   waits=[(ch["cc1"].h, half + 1)] if r == 0 else [],
                   incs=[ch["x2f"].inc(16)])
    for g in range(2):
        P.emit("gpsimd", "dma_start", out=PW[:, g * 2048:(g + 1) * 2048],
               in_=d_wav[g, :, :],
               waits=list(sbr) if g == 0 else [], incs=[ch["pw"].inc(16)])
    for g in range(2):
        P.emit("gpsimd", "dma_start",
               out=PW[:, (2 + g) * 2048:(3 + g) * 2048],
               in_=d_wak[g, :, :], incs=[ch["pw"].inc(16)])

    x2f_ok = ch["x2f"].at()
    x2f_b = ch["x2f"].n - 64
    pw_ok = ch["pw"].at()

    for g in range(2):
        for tcg in range(4):
            pb = (PBA, PBB)[tcg % 2]
            idx = g * 4 + tcg
            w = [pw_ok, ch["tblld"].at(), ch["const"].at()] \
                if idx == 0 else []
            if idx >= 2:
                w.append((ch["ropek"].h, idx - 1))
            for k in range(NK):
                wk = list(w) if k == 0 else []
                if idx == 0 and k in (0, 8):
                    wk.append((ch["x2f"].h, x2f_b + 32 * (k // 8 + 1)))
                P.emit("tensor", "matmul", pb[:, :],
                       PW[:, (2 + g) * 2048 + k * 128:
                          (2 + g) * 2048 + (k + 1) * 128],
                       X2F[:, k * TG + tcg * 512:k * TG + (tcg + 1) * 512],
                       start=(k == 0), stop=(k == NK - 1),
                       waits=wk,
                       incs=[ch["kmm"].inc()] if k == NK - 1 else [])
            emit_rope(pb, K_f[:, g * TG + tcg * 512:g * TG + (tcg + 1) * 512],
                      tcg * 512, SCR1, SCR2, [ch["kmm"].at()],
                      [ch["ropek"].inc()])

    for tt in range(NQT):
        pc = (PBC, PBD)[tt % 2]
        for mc in range(NK):
            w = []
            if tt == 0 and mc == 0:
                w = [pw_ok, (ch["x2f"].h, x2f_b + 32)]
            if tt == 0 and mc == 8:
                w = [(ch["x2f"].h, x2f_b + 64)]
            if mc == 0 and tt >= 2:
                w.append((ch["vev"].h, tt - 1))
            for g in range(2):
                P.emit("tensor", "matmul", pc[:, g * 128:(g + 1) * 128],
                       X2F[:, mc * TG + tt * 128:mc * TG + (tt + 1) * 128],
                       PW[:, g * 2048 + mc * 128:g * 2048 + (mc + 1) * 128],
                       start=(mc == 0 and g == 0),
                       stop=(mc == NK - 1 and g == 1),
                       waits=w if g == 0 else [],
                       incs=[ch["vmm"].inc()]
                       if (g == 1 and mc == NK - 1) else [])
        P.emit("scalar", "copy", VT_f[:, tt * 256:(tt + 1) * 256],
               pc[:, :], waits=[ch["vmm"].at()], incs=[ch["vev"].inc()])

    for h in range(8):
        slot = 2 + h % 2
        w = [(ch["kmm"].h, 4 * (slot - 1))] if h < 2 else \
            [(ch["qmm"].h, 4 * (h - 1))]
        P.emit("gpsimd", "dma_start",
               out=PW[:, slot * 2048:(slot + 1) * 2048],
               in_=d_waq[h, :, :], waits=w, incs=[ch["pw"].inc(16)])
        for qc in range(4):
            pb = (PBA, PBB)[qc % 2]
            idx = h * 4 + qc
            w = [(ch["pw"].h, 16 * (4 + h + 1))] if qc == 0 else []
            if idx == 0:
                w.append(ch["ropek"].at())
            if idx >= 2:
                w.append((ch["ropeq"].h, idx - 1))
            for k in range(NK):
                P.emit("tensor", "matmul", pb[:, :],
                       PW[:, slot * 2048 + k * 128:slot * 2048 + (k + 1) * 128],
                       X2F[:, k * TG + qc * 512:k * TG + (qc + 1) * 512],
                       start=(k == 0), stop=(k == NK - 1),
                       waits=w if k == 0 else [],
                       incs=[ch["qmm"].inc()] if k == NK - 1 else [])
            emit_rope(pb, Q[:, h * TG + qc * 512:h * TG + (qc + 1) * 512],
                      qc * 512, SCR1, SCR2, [ch["qmm"].at()],
                      [ch["ropeq"].inc()])

    tap("QKV", [ch["ropeq"].at(), ch["ropek"].at(), ch["vev"].at()])
    if stage == "QKV":
        return _build([ch["ropeq"].at(), ch["ropek"].at(), ch["vev"].at()])
    psB.close()
    esB2.close()

    esC = ExitStack()
    X2 = esC.enter_context(nc.sbuf_tensor("X2", [128, NK * TQ], F32))
    PT = [esC.enter_context(nc.sbuf_tensor(f"PT{i}", [128, 2048], BF16))
          for i in range(2)]
    SCRAP = esC.enter_context(nc.sbuf_tensor("SCRAP", [128, 1024], BF16))
    NMBH = esC.enter_context(nc.sbuf_tensor("NMBH", [1, 2048], BF16))
    NMBL = esC.enter_context(nc.sbuf_tensor("NMBL", [1, 2048], BF16))
    OTR = [esC.enter_context(nc.sbuf_tensor(f"OTR{i}", [128, 2048], BF16))
           for i in range(2)]
    LOHI = [esC.enter_context(nc.sbuf_tensor(f"LOHI{i}", [128, 8192], BF16))
            for i in range(2)]
    WAOB = [esC.enter_context(nc.sbuf_tensor(f"WAOB{i}", [128, 2048], BF16))
            for i in range(5)]
    psC = ExitStack()
    PS_A = psC.enter_context(nc.psum_tensor("PS_A", [128, 2048], F32))
    PS_B = psC.enter_context(nc.psum_tensor("PS_B", [128, 2048], F32))
    bufs.update(X2=X2, OTR0=OTR[0], OTR1=OTR[1], s_nmla=s_nmla, NMBH=NMBH,
                NMBL=NMBL, s_nm=s_nm, s_nm2=s_nm2, s_rs=s_rs, s_rs2=s_rs2,
                s_nmm=s_nmm, s_d1=s_d1, s_d2=s_d2, s_e1=s_e1, s_e2=s_e2,
                s_lnr=s_lnr, s_dbg=s_dbg)

    scr = [ch["ropeq"].at(), ch["vev"].at(), ch["qmm"].at(),
           ch["ropek"].at()]
    P.emit("sync", "dma_start", out=X2[:, :], in_=d_xpark[:, :],
           waits=list(scr), incs=[ch["xrel"].inc(16)])
    for s in range(4):
        P.emit("gpsimd", "dma_start", out=WAOB[s][:, :], in_=d_wao[s, :, :],
               waits=list(scr) if s == 0 else [],
               incs=[ch["waob"].inc(16)])

    ident_f = s_cst[:, C_ID:C_ID + 128]
    mask_fb = s_cbf[:, 128:256]
    mask_tb = s_cbf[:, 256:384]
    ident_bf = s_cbf[:, 0:128]
    onerow = s_crow[0:1, 128:256]

    onesbf = s_cbf[0:1, 392:520]
    NMROW = RA[0:1, 0:2048]

    def REG(i):
        return PS_A[:, (i % 2) * 1024:(i % 2 + 1) * 1024]

    segi = [0]
    prev_bias_done = [None]
    lnrow_base = {}
    nmb_ready = {}
    pv_after = {}

    def seg_wait():
        w = []
        if segi[0] >= 2:
            w.append((ch["seg"].h, segi[0] - 1))
        return w

    def REG(i):
        return PS_A[:, (i % 2) * 1024:(i % 2 + 1) * 1024]

    def emit_spass_qt(hh, qt):
        glh = hh // 4
        if qt == 0:
            lnrow_base[hh] = ch["lnrow"].n
        kw = (qt + 1) * 128
        nseg = 1 if kw <= 1024 else 2
        for sg in range(nseg):
            a = sg * 1024
            ws = min(kw, 1024) if sg == 0 else kw - 1024
            reg = REG(segi[0])
            w = seg_wait()
            if hh == 0 and qt == 0:
                w += list(scr) + [ch["cbf"].at()]
            last_seg = (sg == nseg - 1)
            nfull, rem = divmod(ws, 512)
            nch = nfull + (1 if rem else 0)
            for c in range(nch):
                n = 512 if c < nfull else rem
                last_ch = (c == nch - 1)
                P.emit("tensor", "matmul", reg[:, c * 512:c * 512 + n],
                       Q[:, hh * TG + qt * 128:hh * TG + (qt + 1) * 128],
                       K_f[:, glh * TG + a + c * 512:
                           glh * TG + a + c * 512 + n],
                       start=True,
                       stop=(not (last_seg and last_ch)),
                       waits=w if c == 0 else [],
                       incs=([ch["smm"].inc()]
                             if (last_ch and not last_seg) else []))
            if last_seg:
                m0 = ws - 128
                P.emit("tensor", "matmul", reg[:, m0:m0 + 128], ident_bf,
                       mask_fb, start=False, stop=True,
                       incs=[ch["smm"].inc()])
            nmdst = s_nm if sg == 0 else s_nm2
            rsdst = s_rs if sg == 0 else s_rs2
            P.emit("vector", "reduce_max", nmdst[:, :], reg[:, 0:ws],
                   axis=AX.X, negate=True,
                   waits=[ch["smm"].at(), ch["seg"].at()],
                   incs=[ch["negmax"].inc()])
            P.emit("scalar", "activation", SCRAP[:, 0:ws], reg[:, 0:ws],
                   AF.Exp, bias=nmdst[:, :], accum_out=rsdst[:, :],
                   waits=[ch["negmax"].at()], incs=[ch["seg"].inc()])
            if nseg == 2:
                P.emit("scalar", "copy", s_dbg[:, sg:sg + 1], rsdst[:, :])
            segi[0] += 1
        if nseg == 1:
            P.emit("scalar", "activation", s_lnr[:, :], s_rs[:, :],
                   AF.Ln, incs=[ch["mg2"].inc()])
            P.emit("vector", "tensor_sub", s_nmla[:, qt:qt + 1],
                   s_nm[:, :], s_lnr[:, :],
                   waits=[ch["mg2"].at()], incs=[ch["lnrow"].inc()])
        else:
            P.emit("vector", "tensor_sub", s_d1[:, :], s_nm[:, :],
                   s_nm2[:, :], waits=[ch["negmax"].at()])
            P.emit("vector", "tensor_sub", s_d2[:, :], s_nm2[:, :],
                   s_nm[:, :], incs=[ch["mg1"].inc()])
            P.emit("scalar", "activation", s_e1[:, :], s_d1[:, :],
                   AF.Relu, waits=[ch["mg1"].at()])
            P.emit("scalar", "activation", s_e2[:, :], s_d2[:, :],
                   AF.Relu)
            P.emit("scalar", "activation", s_d1[:, :], s_e1[:, :],
                   AF.Exp, scale=s_cst[:, C_NEG1:C_NEG1 + 1])
            P.emit("scalar", "activation", s_d2[:, :], s_e2[:, :],
                   AF.Exp, scale=s_cst[:, C_NEG1:C_NEG1 + 1])
            P.emit("scalar", "mul", s_rs[:, :], s_d1[:, :],
                   s_dbg[:, 0:1])
            P.emit("scalar", "mul", s_rs2[:, :], s_d2[:, :],
                   s_dbg[:, 1:2], incs=[ch["mg1"].inc()])
            P.emit("vector", "tensor_add", s_rs[:, :], s_rs[:, :],
                   s_rs2[:, :], waits=[ch["mg1"].at()],
                   incs=[ch["mg2"].inc()])
            P.emit("scalar", "activation", s_lnr[:, :], s_rs[:, :],
                   AF.Ln, waits=[ch["mg2"].at()])
            P.emit("scalar", "add", s_e2[:, :], s_lnr[:, :], s_e1[:, :],
                   incs=[ch["mg2"].inc()])
            P.emit("vector", "tensor_sub", s_nmla[:, qt:qt + 1],
                   s_nm[:, :], s_e2[:, :],
                   waits=[ch["mg2"].at()], incs=[ch["lnrow"].inc()])

    def emit_transposes_nmb(hh):
        for qt in range(NQT):
            wt = [(ch["lnrow"].h, lnrow_base[hh] + qt + 1)]
            if qt == 0:
                wt.append(ch["otev"].at())
            P.emit("tensor", "transpose",
                   PS_B[0:1, qt * 128:(qt + 1) * 128],
                   s_nmla[:, qt:qt + 1], ident_f, waits=wt,
                   incs=[ch["nmtr"].inc()] if qt == NQT - 1 else [])
        wb = [ch["nmtr"].at()]
        if prev_bias_done[0] is not None:
            wb.append(prev_bias_done[0])
        P.emit("vector", "tensor_copy", NMBH[0:1, :], PS_B[0:1, :], waits=wb)
        P.emit("vector", "tensor_copy", RA[0:1, 0:2048], NMBH[0:1, :])
        P.emit("vector", "tensor_sub", NMBL[0:1, :], PS_B[0:1, :],
               RA[0:1, 0:2048], incs=[ch["nmev"].inc(2)])
        nmb_ready[hh] = ch["nmev"].at()

    def emit_sT_j(hh, j):
        glh = hh // 4
        width = (NQT - j) * 128
        nseg = 2 if width > 1024 else 1
        for sg in range(nseg):
            qa = sg * 1024
            ws = (min(width, 1024) if sg == 0 else width - 1024)
            reg = REG(segi[0])
            w = seg_wait()
            if j == 0 and sg == 0:
                w.append(nmb_ready[hh])
            nfull, rem = divmod(ws, 512)
            nch = nfull + (1 if rem else 0)
            has_mask = (sg == 0)
            for c in range(nch):
                n = 512 if c < nfull else rem
                boff = j * 128 + qa + c * 512
                P.emit("tensor", "matmul", reg[:, c * 512:c * 512 + n],
                       onesbf, NMBH[0:1, boff:boff + n],
                       start=True, stop=False, waits=w if c == 0 else [])
                P.emit("tensor", "matmul", reg[:, c * 512:c * 512 + n],
                       onesbf, NMBL[0:1, boff:boff + n],
                       start=False, stop=False)
                P.emit("tensor", "matmul", reg[:, c * 512:c * 512 + n],
                       K_f[:, glh * TG + j * 128:glh * TG + (j + 1) * 128],
                       Q[:, hh * TG + boff:hh * TG + boff + n],
                       start=False, stop=(not (has_mask and c == 0)),
                       incs=([ch["smm"].inc()]
                             if (not has_mask and c == nch - 1) else []))
            if has_mask:
                P.emit("tensor", "matmul", reg[:, 0:128], ident_bf,
                       mask_tb, start=False, stop=True,
                       incs=[ch["smm"].inc()])
            we = [ch["smm"].at()]
            if j >= 2 and sg == 0:
                we.append(ch["pv"].at())
            P.emit("scalar", "activation", PT[j % 2][:, qa:qa + ws],
                   reg[:, 0:ws], AF.Exp,
                   waits=we, incs=[ch["seg"].inc()])
            segi[0] += 1
        pv_after[(hh, j)] = segi[0]

    def emit_pv(hh, j):
        glh = hh // 4
        nq = NQT - j
        wpv = [(ch["seg"].h, pv_after[(hh, j)])]
        for u in range(nq):
            qt = j + u
            P.emit("tensor", "matmul",
                   PS_B[:, qt * 128:(qt + 1) * 128],
                   VT_f[:, (j * 2 + glh) * 128:(j * 2 + glh + 1) * 128],
                   PT[j % 2][:, u * 128:(u + 1) * 128],
                   start=(j == 0 and qt % 4 == 0),
                   stop=(j == qt and qt % 4 == 3),
                   waits=wpv if u == 0 else [],
                   incs=[ch["pv"].inc()] if u == nq - 1 else [])

    s0 = s_cst[:, C_S0:C_S0 + 1]
    s1 = s_cst[:, C_S1:C_S1 + 1]
    order = [0, 2, 1, 3]

    def oprj_unpack(si):
        grp = order[si]
        w = [(ch["cc2"].h, 1 if grp in (0, 2) else 2)]
        if si >= 2:
            w.append((ch["opj"].h, 32 * (si - 1)))
        for i in range(4):
            hg = grp * 4 + i
            if grp in (0, 2):
                srcT = d_x2out_a
                row0 = (512 if grp == 2 else 0) + (hg % 8) * 128
            else:
                srcT = d_x2out_b
                row0 = (512 if grp == 3 else 0) + ((hg - 4) % 8) * 128
            for part in range(2):
                P.emit("sync", "dma_start",
                       out=LOHI[si % 2][:, i * 2048 + part * TQ:
                                        i * 2048 + (part + 1) * TQ],
                       in_=srcT[row0:row0 + 128,
                                part * TQ:(part + 1) * TQ],
                       waits=w if (i == 0 and part == 0) else [],
                       incs=[ch["lohi"].inc(16)])

    def oprj_blend(si):
        wbl = [(ch["lohi"].h, 16 * 8 * (si + 1))]
        if si >= 2:
            wbl.append((ch["opj"].h, 32 * (si - 1)))
        for i in range(4):
            L = LOHI[si % 2][:, i * 2048:i * 2048 + TQ]
            H = LOHI[si % 2][:, i * 2048 + TQ:(i + 1) * 2048]
            P.emit("vector", "tensor_scalar_mul", L, L, s0,
                   waits=wbl if i == 0 else [])
            P.emit("vector", "scalar_tensor_tensor", L, H, s1, L,
                   op0=ALU.mult, op1=ALU.add,
                   incs=[ch["blend"].inc()] if i == 3 else [])


    for qt in range(NQT):
        emit_spass_qt(0, qt)
    emit_transposes_nmb(0)
    for h in range(8):
        for j in range(NQT):
            emit_sT_j(h, j)
            if h < 7:
                emit_spass_qt(h + 1, j)
            if j >= 1:
                emit_pv(h, j - 1)
        emit_pv(h, NQT - 1)
        prev_bias_done[0] = (ch["smm"].h, ch["smm"].n)
        w = [ch["pv"].at()]
        if h >= 2:
            w.append((ch["otdma"].h, 16 * (h - 1)))
        P.emit("scalar", "copy", OTR[h % 2][:, :], PS_B[:, :],
               waits=w, incs=[ch["otev"].inc()])
        P.emit("sync", "dma_start", out=d_x2in[h * 128:(h + 1) * 128, :],
               in_=OTR[h % 2][:, :], waits=[ch["otev"].at()],
               incs=[ch["otdma"].inc(16)])
        if h == 3:
            P.emit("gpsimd", "collective_compute", "AllGather", ALU.bypass,
                   replica_groups=[[0, 1], [2, 3], [4, 5], [6, 7]],
                   ins=[d_x2in[0:512, :].opt()],
                   outs=[d_x2out_a.ap().opt()],
                   waits=[(ch["otdma"].h, 64)], incs=[ch["cc2"].inc()])
        if h < 7:
            emit_transposes_nmb(h + 1)
        if h == 6:
            oprj_unpack(0)
            oprj_blend(0)

    tap("ATTN", [ch["otdma"].at()])
    if stage == "ATTN":
        return _build([ch["otdma"].at()])
    P.emit("gpsimd", "collective_compute", "AllGather", ALU.bypass,
           replica_groups=[[0, 1], [2, 3], [4, 5], [6, 7]],
           ins=[d_x2in[512:1024, :].opt()],
           outs=[d_x2out_b.ap().opt()],
           waits=[ch["otdma"].at()], incs=[ch["cc2"].inc()])

    for si, grp in enumerate(order):
        if si < 3:
            for i in range(4):
                sg_strip = order[si + 1] * 4 + i
                s_pos = (si + 1) * 4 + i
                w2 = []
                if s_pos >= 5:
                    w2 = [(ch["opj"].h, 32 * ((s_pos - 5) // 4 + 1))]
                P.emit("gpsimd", "dma_start",
                       out=WAOB[s_pos % 5][:, :],
                       in_=d_wao[sg_strip, :, :], waits=w2,
                       incs=[ch["waob"].inc(16)])
            oprj_unpack(si + 1)
            oprj_blend(si + 1)
        for mc in range(NK):
            for tc in range(2):
                pp = (mc * 2 + tc) % 2
                pr = PS_A[:, pp * 512:(pp + 1) * 512]
                idx = si * 32 + mc * 2 + tc
                w3 = []
                if idx == 0:
                    w3 = [ch["expT"].at(), (ch["blend"].h, si + 1),
                          (ch["waob"].h, 16 * 4 * (si + 1)),
                          ch["xrel"].at()]
                elif mc == 0 and tc == 0:
                    w3 = [(ch["blend"].h, si + 1),
                          (ch["waob"].h, 16 * 4 * (si + 1))]
                if idx >= 2:
                    w3.append((ch["opj_ev"].h, idx - 1))
                for i in range(4):
                    P.emit("tensor", "matmul", pr,
                           WAOB[(si * 4 + i) % 5][:, mc * 128:(mc + 1) * 128],
                           LOHI[si % 2][:, i * 2048 + tc * 512:
                                        i * 2048 + tc * 512 + 512],
                           start=(i == 0), stop=(i == 3),
                           waits=w3 if i == 0 else [],
                           incs=[ch["opj"].inc()] if i == 3 else [])
                P.emit("vector", "tensor_add",
                       X2[:, mc * TQ + tc * 512:mc * TQ + (tc + 1) * 512],
                       X2[:, mc * TQ + tc * 512:mc * TQ + (tc + 1) * 512],
                       pr, waits=[ch["opj"].at()],
                       incs=[ch["opj_ev"].inc()])
    tap("OPRJ", [ch["opj_ev"].at()])
    if stage == "OPRJ":
        return _build([ch["opj_ev"].at()])
    for k in range(NK):
        P.emit("sync", "dma_start", out=d_xpark[:, k * TQ:(k + 1) * TQ],
               in_=X2[:, k * TQ:(k + 1) * TQ],
               waits=[(ch["opj_ev"].h, 96 + 2 * (k + 1))],
               incs=[ch["park2"].inc(16)])

    psC.close()
    esC.close()
    esB.close()

    esD = ExitStack()
    X3 = esD.enter_context(nc.sbuf_tensor("X3", [128, NK * TQ], F32))
    XN3 = esD.enter_context(nc.sbuf_tensor("XN3", [128, NK * TQ], BF16))
    PROD3 = esD.enter_context(nc.sbuf_tensor("PROD3", [128, HHALF * TQ],
                                             BF16))
    WGU3 = [esD.enter_context(nc.sbuf_tensor(f"WGU3{i}", [128, 4096], BF16))
            for i in range(3)]
    WD3 = [esD.enter_context(nc.sbuf_tensor(f"WD3{i}", [128, 2816], BF16))
           for i in range(2)]
    XSQ3 = [esD.enter_context(nc.sbuf_tensor(f"XSQ3{i}", [128, TQ], F32))
            for i in range(2)]
    SG3 = [esD.enter_context(nc.sbuf_tensor(f"SG3{i}", [128, TQ], F32))
           for i in range(2)]
    psD = ExitStack()
    PP3 = [psD.enter_context(nc.psum_tensor(f"PP3{i}", [128, 2048], F32))
           for i in range(2)]
    bufs.update(X3=X3, XN3=XN3)

    for k in range(NK):
        P.emit("sync", "dma_start", out=X3[:, k * TQ:(k + 1) * TQ],
               in_=d_xpark[:, k * TQ:(k + 1) * TQ],
               waits=[(ch["park2"].h, 16 * (k + 1))],
               incs=[ch["xrel2"].inc(16)])
    ffn_prefetch(WGU3, 1, dma_first_waits=[ch["opj"].at(),
                                            ch["park2"].at(),
                                            ch["otdma"].at()])
    emit_ln(X3, XN3, XSQ3, SG3, PP3[0], PP3[1],
            lambda k: [(ch["xrel2"].h, 16 * (k + 1))],
            k0_extra=[ch["otev"].at()],
            bcast_extra=[ch["opj_ev"].at()])
    tap("L3", [ch["xn"].at()])
    if stage == "L3":
        return _build([ch["xn"].at()])
    emit_ffn(X3, XN3, PROD3, WGU3, WD3, SG3, PP3, 1,
             pp1_waits=[ch["otev"].at()],
             dma_first_waits=[ch["opj"].at(), ch["park2"].at(),
                              ch["otdma"].at()], prologue_done=True)
    tap("F2", [ch["dn_ev"].at()])

    P.emit("sync", "dma_start",
           out=d_y.ap().rearrange("(k p) t -> p k t", p=128),
           in_=X3[:, :].rearrange("p (k t) -> p k t", k=NK),
           waits=[ch["dn_ev"].at()], incs=[ch["out"].inc(16)])
    psD.close()
    esD.close()
    return _build([ch["out"].at()])


def _prep_inputs(L0_attn_x, L0_attn_w_aq, L0_attn_w_ak, L0_attn_w_av,
                 L0_attn_w_ao, L0_ffn_wq_gate, L0_ffn_s_gate, L0_ffn_wq_up,
                 L0_ffn_s_up, L0_ffn_wq_down, L0_ffn_s_down):
    x = np.asarray(L0_attn_x, dtype=np.float32)
    wq_gate = np.asarray(L0_ffn_wq_gate, dtype=np.int32)
    wq_up = np.asarray(L0_ffn_wq_up, dtype=np.int32)
    wq_down = np.asarray(L0_ffn_wq_down, dtype=np.int32)
    w_aq = np.asarray(L0_attn_w_aq, dtype=np.float32)
    w_ak = np.asarray(L0_attn_w_ak, dtype=np.float32)
    w_av = np.asarray(L0_attn_w_av, dtype=np.float32)
    w_ao = np.asarray(L0_attn_w_ao, dtype=np.float32)
    s_g = np.asarray(L0_ffn_s_gate, dtype=np.float32)
    s_u = np.asarray(L0_ffn_s_up, dtype=np.float32)
    s_d = np.asarray(L0_ffn_s_down, dtype=np.float32)

    g_t = wq_gate.reshape(NH, 128, NK, 128).transpose(0, 3, 2, 1) \
        .reshape(NH, 128, 2048)
    u_t = wq_up.reshape(NH, 128, NK, 128).transpose(0, 3, 2, 1) \
        .reshape(NH, 128, 2048)
    wgu_t = np.ascontiguousarray(np.concatenate([g_t, u_t], axis=2))
    wd_t = np.ascontiguousarray(
        wq_down.reshape(NDO, 128, NH, 128).transpose(0, 3, 2, 1)
        .reshape(NDO, 128, 5632))
    waq_full = np.ascontiguousarray(
        w_aq.reshape(4, 4, NK, 128, 128).transpose(0, 1, 3, 2, 4)
        .reshape(16, 128, 2048))
    wak_full = np.ascontiguousarray(
        w_ak.reshape(4, NK, 128, 128).transpose(0, 2, 1, 3)
        .reshape(4, 128, 2048))
    wav_full = np.ascontiguousarray(
        w_av.reshape(4, NK, 128, 128).transpose(0, 2, 1, 3)
        .reshape(4, 128, 2048))
    wao_t = np.ascontiguousarray(w_ao.reshape(16, 128, 2048))

    mult = np.float32(128.0 ** -0.25)
    pos = np.arange(TG, dtype=np.float32)
    freqs = np.power(np.float32(10000.0),
                     -(np.arange(64, dtype=np.float32) / np.float32(64)))
    ang = freqs[:, None] * pos[None, :]
    cos = np.cos(ang).astype(np.float32) * mult
    sin = np.sin(ang).astype(np.float32) * mult
    tbl = np.ascontiguousarray(
        np.concatenate([np.concatenate([cos, cos], 0),
                        np.concatenate([sin, sin], 0)], 1)).astype(np.float32)

    ident = np.eye(128, dtype=np.float32)
    tri = np.triu(np.ones((128, 128), dtype=np.float32), 1)
    mask_f = tri * np.float32(-1e30)
    mask_t = np.ascontiguousarray(tri.T) * np.float32(-1e30)

    cstrow = np.zeros((1, 256), dtype=np.float32)
    cstrow[0, 0:128] = -128.0
    cstrow[0, 128:256] = 1.0

    in_maps = []
    for c in range(8):
        b, par = c // 2, c % 2
        rows = slice(1024 * par, 1024 * par + 1024)
        x_fm = np.ascontiguousarray(x[b, rows, :].T)
        cst = np.zeros((128, 640), dtype=np.float32)
        cst[:, 512:640] = 1.0
        cst[:, 0:128] = ident
        cst[:, 128:256] = mask_f
        cst[:, 256:384] = mask_t
        cst[:, 384:428] = s_g.reshape(NH, 128).T
        cst[:, 428:472] = s_u.reshape(NH, 128).T
        cst[:, 472:488] = s_d.reshape(NDO, 128).T
        cst[:, 488] = 1.0
        cst[:, 489] = 1.0 - par
        cst[:, 490] = float(par)
        cst[:, 491] = EPS
        cst[:, 492] = -0.5
        cst[:, 493] = -1.0
        in_maps.append(dict(
            x_fm=x_fm, wgu_t=wgu_t, wd_t=wd_t,
            waq_t=np.ascontiguousarray(waq_full[8 * par:8 * par + 8]),
            wak_t=np.ascontiguousarray(wak_full[2 * par:2 * par + 2]),
            wav_t=np.ascontiguousarray(wav_full[2 * par:2 * par + 2]),
            wao_t=wao_t, cst=cst, cstrow=cstrow, tbl=tbl))
    return in_maps


def kernel(**inputs):
    in_maps = _prep_inputs(**inputs)
    nc = build_nc(taps=_TAPS)
    res = run_bass_kernel_spmd(nc, in_maps, core_ids=list(range(8)))
    out = np.empty((4, 2048, 2048), dtype=np.float32)
    for c in range(8):
        b, par = c // 2, c % 2
        rows = slice(1024 * par, 1024 * par + 1024)
        out[b, rows, :] = res.results[c]["y_fm"].T
    kernel._last_results = res
    return out


# version 13
# speedup vs baseline: 1.0920x; 1.0920x over previous
import numpy as np
import concourse.bass as bass
import concourse.mybir as mybir
from concourse.bass_utils import run_bass_kernel_spmd
from contextlib import ExitStack

F32 = mybir.dt.float32
BF16 = mybir.dt.bfloat16
I32 = mybir.dt.int32
AF = mybir.ActivationFunctionType
ALU = mybir.AluOpType
AX = mybir.AxisListType

NK = 16
NH = 44
HHALF = 22
NDO = 16
TQ = 1024
TG = 2048
NQT = 16
EPS = 1e-5

_TAPS = []


class Ch:
    def __init__(self, nc, es, name):
        self.h = es.enter_context(nc.semaphore(name))
        self.n = 0

    def inc(self, k=1):
        self.n += k
        return (self.h, k)

    def at(self, v=None):
        v = self.n if v is None else v
        assert v <= self.n, "wait on future inc -> deadlock"
        return (self.h, v)


class Prog:
    def __init__(self):
        self.ops = {k: [] for k in ("tensor", "vector", "scalar", "sync",
                                    "gpsimd")}

    def emit(self, eng, meth, *args, waits=(), incs=(), **kwargs):
        self.ops[eng].append((meth, args, kwargs, list(waits), list(incs)))


def _run_engine(eng, oplist):
    for meth, args, kwargs, waits, incs in oplist:
        for h, v in waits:
            eng.wait_ge(h, v)
        if meth is None:
            continue
        ins = getattr(eng, meth)(*args, **kwargs)
        for h, k in incs:
            ins.then_inc(h, k)


def build_nc(taps=(), stage="full", detect_races=True):
    nc = bass.Bass("TRN2", target_bir_lowering=False,
                   detect_race_conditions=detect_races)
    P = Prog()
    es = ExitStack()

    d_x = nc.dram_tensor("x_fm", [2048, TQ], F32, kind="ExternalInput")
    d_wgu = nc.dram_tensor("wgu_t", [NH, 128, 4096], I32, kind="ExternalInput")
    d_wd = nc.dram_tensor("wd_t", [NDO, 128, 5632], I32, kind="ExternalInput")
    d_waq = nc.dram_tensor("waq_t", [8, 128, 2048], F32, kind="ExternalInput")
    d_wak = nc.dram_tensor("wak_t", [2, 128, 2048], F32, kind="ExternalInput")
    d_wav = nc.dram_tensor("wav_t", [2, 128, 2048], F32, kind="ExternalInput")
    d_wao = nc.dram_tensor("wao_t", [16, 128, 2048], F32, kind="ExternalInput")
    d_cst = nc.dram_tensor("cst", [128, 640], F32, kind="ExternalInput")
    d_crow = nc.dram_tensor("cstrow", [1, 256], F32, kind="ExternalInput")
    d_tbl = nc.dram_tensor("tbl", [128, 4096], F32, kind="ExternalInput")
    d_y = nc.dram_tensor("y_fm", [2048, TQ], F32, kind="ExternalOutput")

    d_xpark = nc.dram_tensor("xpark", [128, NK * TQ], F32, kind="Internal")
    d_x1in = nc.dram_tensor("xch1_in", [2048, TQ], BF16, kind="Internal")
    d_x1out_a = nc.dram_tensor("xch1_out_a", [2048, TQ], BF16,
                               kind="Internal")
    d_x1out_b = nc.dram_tensor("xch1_out_b", [2048, TQ], BF16,
                               kind="Internal")
    d_x2in = nc.dram_tensor("xch2_in", [1024, TG], BF16, kind="Internal")
    d_x2out_a = nc.dram_tensor("xch2_out_a", [1024, TG], BF16,
                               kind="Internal")
    d_x2out_b = nc.dram_tensor("xch2_out_b", [1024, TG], BF16,
                               kind="Internal")

    C_ID, C_MF, C_MT, C_SG, C_SU, C_SD, C_ONE, C_S0, C_S1 = (
        0, 128, 256, 384, 428, 472, 488, 489, 490)
    C_EPS, C_NHALF, C_NEG1 = 491, 492, 493

    s_cst = es.enter_context(nc.sbuf_tensor("s_cst", [128, 640], F32))
    s_cbf = es.enter_context(nc.sbuf_tensor("s_cbf", [128, 520], BF16))
    s_crow = es.enter_context(nc.sbuf_tensor("s_crow", [1, 256], F32))
    RA = es.enter_context(nc.sbuf_tensor("rows_a", [128, 2 * TQ], F32))
    s_nm = es.enter_context(nc.sbuf_tensor("s_negmax", [128, 1], F32))
    s_rs = es.enter_context(nc.sbuf_tensor("s_rowsum", [128, 1], F32))
    s_lnr = es.enter_context(nc.sbuf_tensor("s_lnr", [128, 1], F32))
    s_nmla = es.enter_context(nc.sbuf_tensor("s_nmls_all", [128, 16], F32))
    s_nm2 = es.enter_context(nc.sbuf_tensor("s_negmax2", [128, 1], F32))
    s_rs2 = es.enter_context(nc.sbuf_tensor("s_rowsum2", [128, 1], F32))
    s_nmm = es.enter_context(nc.sbuf_tensor("s_nmm", [128, 1], F32))
    s_d1 = es.enter_context(nc.sbuf_tensor("s_d1", [128, 1], F32))
    s_d2 = es.enter_context(nc.sbuf_tensor("s_d2", [128, 1], F32))
    s_e1 = es.enter_context(nc.sbuf_tensor("s_e1", [128, 1], F32))
    s_e2 = es.enter_context(nc.sbuf_tensor("s_e2", [128, 1], F32))
    s_dbg = es.enter_context(nc.sbuf_tensor("s_dbg", [128, 4], F32))

    bufs = {}
    ch = {}
    for name in ("const", "cbf", "xload", "sq", "lnst", "lnrow", "lnbc", "xn",
                 "wgu_ld", "wgu_dn", "wgu_c", "wd_c", "ffn_mm", "silu", "prod", "cs_mm", "cs",
                 "ps_mm", "ps", "wd_ld", "wd_dn", "dn_mm", "dn_ev",
                 "park1", "pk1", "cc1", "x2f", "pw", "kmm", "ropek", "vmm",
                 "vev", "qmm", "ropeq", "tblld", "negmax",
                 "expS", "nmtr", "nmev", "expT", "pv", "otev", "otdma", "cc2",
                 "seg", "smm", "mg1", "mg2",
                 "lohi", "blend", "waob", "opj", "opj_ev", "park2",
                 "xrel", "xrel2", "out", "dbg"):
        ch[name] = Ch(nc, es, "c_" + name)

    def tap(tag, waits):
        for t, bname in taps:
            if t != tag:
                continue
            buf = bufs[bname]
            o = nc.dram_tensor(f"dbg_{tag}_{bname}",
                               [buf.shape[0], buf.shape[1]], buf.dtype,
                               kind="ExternalOutput")
            P.emit("sync", "dma_start", out=o[:, :], in_=buf[:, :],
                   waits=list(waits), incs=[ch["dbg"].inc(16)])

    def _build(final_waits):
        P.emit("sync", None, waits=list(final_waits) + [ch["dbg"].at()])
        with nc.Block() as block:
            @block.tensor
            def _(eng):
                _run_engine(eng, P.ops["tensor"])

            @block.vector
            def _(eng):
                _run_engine(eng, P.ops["vector"])

            @block.scalar
            def _(eng):
                _run_engine(eng, P.ops["scalar"])

            @block.sync
            def _(eng):
                _run_engine(eng, P.ops["sync"])

            @block.gpsimd
            def _(eng):
                _run_engine(eng, P.ops["gpsimd"])
        return nc

    def emit_ln(X, XN, XSQ, SG, PPa, PPb, chunk_wait, k0_extra=(),
                bcast_extra=()):
        onecol = s_cst[:, C_ONE:C_ONE + 1]
        onerow = s_crow[0:1, 128:256]
        r_sx, r_sxx = RA[0:1, 0:TQ], RA[0:1, TQ:2 * TQ]
        for k in range(NK):
            w = list(chunk_wait(k)) + (list(k0_extra) if k == 0 else [])
            if k >= 2:
                w.append((ch["lnst"].h, ch["lnst"].n - 1))
            P.emit("scalar", "square", XSQ[k % 2][:, :],
                   X[:, k * TQ:(k + 1) * TQ], waits=w, incs=[ch["sq"].inc()])
            wpe = list(chunk_wait(k)) + (list(k0_extra) if k == 0 else []) \
                + [ch["sq"].at()]
            for tc in range(2):
                P.emit("tensor", "matmul",
                       PPb[0:1, TQ + tc * 512:TQ + (tc + 1) * 512], onecol,
                       X[:, k * TQ + tc * 512:k * TQ + (tc + 1) * 512],
                       start=(k == 0), stop=(k == NK - 1),
                       waits=wpe if tc == 0 else [])
                P.emit("tensor", "matmul",
                       PPa[0:1, TQ + tc * 512:TQ + (tc + 1) * 512], onecol,
                       XSQ[k % 2][:, tc * 512:(tc + 1) * 512],
                       start=(k == 0), stop=(k == NK - 1),
                       incs=[ch["lnst"].inc()] if tc == 1 else [])
        P.emit("vector", "tensor_copy", r_sx, PPb[0:1, TQ:2 * TQ],
               waits=[ch["lnst"].at()])
        P.emit("vector", "tensor_copy", r_sxx, PPa[0:1, TQ:2 * TQ],
               incs=[ch["lnrow"].inc()])
        for tc in range(2):
            P.emit("tensor", "matmul", PPa[:, tc * 512:(tc + 1) * 512],
                   onerow, RA[0:1, tc * 512:(tc + 1) * 512],
                   start=True, stop=True,
                   waits=([ch["lnrow"].at()] + list(bcast_extra))
                   if tc == 0 else [])
            P.emit("tensor", "matmul",
                   PPa[:, TQ + tc * 512:TQ + (tc + 1) * 512],
                   onerow, RA[0:1, TQ + tc * 512:TQ + (tc + 1) * 512],
                   start=True, stop=True,
                   incs=[ch["lnbc"].inc()] if tc == 1 else [])
        P.emit("vector", "tensor_scalar_mul", XSQ[0][:, :], PPa[:, 0:TQ],
               1.0 / 2048.0, waits=[ch["lnbc"].at()])
        P.emit("vector", "scalar_tensor_tensor", XSQ[1][:, :], XSQ[0][:, :],
               -1.0, XSQ[0][:, :], op0=ALU.mult, op1=ALU.mult)
        P.emit("vector", "scalar_tensor_tensor", XSQ[1][:, :],
               PPa[:, TQ:2 * TQ], 1.0 / 2048.0, XSQ[1][:, :],
               op0=ALU.mult, op1=ALU.add, incs=[ch["lnrow"].inc()])
        P.emit("scalar", "activation", PPa[:, TQ:2 * TQ], XSQ[1][:, :],
               AF.Ln, bias=s_cst[:, C_EPS:C_EPS + 1],
               waits=[ch["lnrow"].at()])
        P.emit("scalar", "activation", XSQ[1][:, :], PPa[:, TQ:2 * TQ],
               AF.Exp, scale=s_cst[:, C_NHALF:C_NHALF + 1],
               incs=[ch["lnrow"].inc()])
        P.emit("vector", "scalar_tensor_tensor", SG[0][:, :], XSQ[0][:, :],
               -1.0, XSQ[1][:, :], op0=ALU.mult, op1=ALU.mult,
               waits=[ch["lnrow"].at()], incs=[ch["lnrow"].inc()])
        for k in range(NK):
            w = [ch["lnrow"].at()] if k == 0 else []
            P.emit("vector", "tensor_mul", SG[1][:, :],
                   X[:, k * TQ:(k + 1) * TQ], XSQ[1][:, :], waits=w)
            P.emit("vector", "tensor_add", XN[:, k * TQ:(k + 1) * TQ],
                   SG[1][:, :], SG[0][:, :],
                   incs=[ch["xn"].inc()])

    def emit_ffn(X, XN, PROD, WGU, WD, SG, PP, pass_idx, pp1_waits=(),
                 dma_first_waits=(), prologue_done=False):
        xn_ready = ch["xn"].at()
        xn_base = ch["xn"].n - NK

        def wgu_dma(hc):
            w = []
            if pass_idx == 0 and hc == 0:
                w = list(dma_first_waits)
            elif pass_idx == 1 and hc == 0:
                w = list(dma_first_waits)
            if pass_idx * NH + hc >= 3:
                w.append((ch["ffn_mm"].h, pass_idx * NH + hc - 2))
            P.emit("gpsimd", "dma_start", out=WGU[hc % 3][:, :],
                   in_=d_wgu[hc, :, :], waits=w, incs=[ch["wgu_ld"].inc(16)])

        def wgu_fix(hc):
            P.emit("vector", "tensor_scalar_add", WGU[hc % 3][:, :],
                   WGU[hc % 3][:, :], -128.0,
                   waits=[(ch["wgu_ld"].h, 16 * (pass_idx * NH + hc + 1))],
                   incs=[ch["wgu_c"].inc()])

        def wd_dma(do, half):
            idx = pass_idx * 32 + half * NDO + do
            w = list(dma_first_waits) if idx == 0 else []
            if idx >= 2:
                w.append((ch["dn_mm"].h, idx - 1))
            P.emit("gpsimd", "dma_start", out=WD[do % 2][:, :],
                   in_=d_wd[do, :, half * 2816:(half + 1) * 2816],
                   waits=w, incs=[ch["wd_ld"].inc(16)])

        def wd_fix(do, half):
            idx = pass_idx * 32 + half * NDO + do
            P.emit("vector", "tensor_scalar_add", WD[do % 2][:, :],
                   WD[do % 2][:, :], -128.0,
                   waits=[(ch["wd_ld"].h, 16 * (idx + 1))],
                   incs=[ch["wd_c"].inc()])

        for half in range(2):
            base = half * HHALF
            if not (half == 0 and prologue_done):
                for hc in range(base, base + 3):
                    wgu_dma(hc)
                wgu_fix(base)
                wgu_fix(base + 1)
            for i, hc in enumerate(range(base, base + HHALF)):
                p = hc % 3
                w = [(ch["wgu_c"].h, pass_idx * NH + hc + 1)]
                if hc < 2:
                    if hc == 1:
                        w += list(pp1_waits)
                else:
                    w.append((ch["silu"].h, pass_idx * NH + hc - 1))
                    w.append((ch["prod"].h, pass_idx * NH + hc - 1))
                if half == 1 and i < 2:
                    w.append((ch["dn_ev"].h, pass_idx * 32 + 16))
                first = True
                for mat in range(2):
                    for tc in range(2):
                        out = PP[hc % 2][:, mat * TQ + tc * 512:
                                         mat * TQ + (tc + 1) * 512]
                        for k in range(NK):
                            wk = list(w) if first else []
                            if hc < 2 and mat == 0 and tc == 0:
                                wk.append((ch["xn"].h, xn_base + k + 1))
                            P.emit("tensor", "matmul", out,
                                   WGU[p][:, mat * 2048 + k * 128:
                                          mat * 2048 + (k + 1) * 128],
                                   XN[:, k * TQ + tc * 512:
                                      k * TQ + (tc + 1) * 512],
                                   start=(k == 0), stop=(k == NK - 1),
                                   waits=wk,
                                   incs=([ch["ffn_mm"].inc()]
                                         if (mat == 1 and tc == 1
                                             and k == NK - 1) else []))
                            first = False
                w = [ch["ffn_mm"].at()]
                if hc >= 2:
                    w.append((ch["prod"].h, pass_idx * NH + hc - 1))
                P.emit("scalar", "activation", SG[hc % 2][:, :],
                       PP[hc % 2][:, 0:TQ], AF.Silu,
                       scale=s_cst[:, C_SG + hc:C_SG + hc + 1],
                       waits=w, incs=[ch["silu"].inc()])
                P.emit("vector", "scalar_tensor_tensor",
                       PROD[:, i * TQ:(i + 1) * TQ],
                       PP[hc % 2][:, TQ:2 * TQ],
                       s_cst[:, C_SU + hc:C_SU + hc + 1], SG[hc % 2][:, :],
                       op0=ALU.mult, op1=ALU.mult,
                       waits=[ch["silu"].at()], incs=[ch["prod"].inc()])
                if hc + 3 < base + HHALF:
                    wgu_dma(hc + 3)
                if hc + 2 < base + HHALF:
                    wgu_fix(hc + 2)
            prod_full = (ch["prod"].h, ch["prod"].n)
            for do in range(2):
                wd_dma(do, half)
            wd_fix(0, half)
            wd_fix(1, half)
            for do in range(NDO):
                idx = pass_idx * 32 + half * NDO + do
                p3 = do % 2
                w = [(ch["wd_c"].h, idx + 1), prod_full]
                if do >= 2:
                    w.append((ch["dn_ev"].h, idx - 1))
                first = True
                for tc in range(2):
                    out = PP[do % 2][:, tc * 512:(tc + 1) * 512]
                    for i in range(HHALF):
                        P.emit("tensor", "matmul", out,
                               WD[p3][:, i * 128:(i + 1) * 128],
                               PROD[:, i * TQ + tc * 512:
                                    i * TQ + (tc + 1) * 512],
                               start=(i == 0), stop=(i == HHALF - 1),
                               waits=w if first else [],
                               incs=([ch["dn_mm"].inc()]
                                     if (tc == 1 and i == HHALF - 1) else []))
                        first = False
                P.emit("vector", "scalar_tensor_tensor",
                       X[:, do * TQ:(do + 1) * TQ], PP[do % 2][:, 0:TQ],
                       s_cst[:, C_SD + do:C_SD + do + 1],
                       X[:, do * TQ:(do + 1) * TQ],
                       op0=ALU.mult, op1=ALU.add,
                       waits=[ch["dn_mm"].at()], incs=[ch["dn_ev"].inc()])
                if do + 2 < NDO:
                    wd_dma(do + 2, half)
                    wd_fix(do + 2, half)

    def ffn_prefetch(WGU, pass_idx, dma_first_waits=()):
        for hc in range(3):
            w = list(dma_first_waits) if hc == 0 else []
            if pass_idx * NH + hc >= 3:
                w.append((ch["ffn_mm"].h, pass_idx * NH + hc - 2))
            P.emit("gpsimd", "dma_start", out=WGU[hc % 3][:, :],
                   in_=d_wgu[hc, :, :], waits=w, incs=[ch["wgu_ld"].inc(16)])
        for hc in range(2):
            P.emit("vector", "tensor_scalar_add", WGU[hc % 3][:, :],
                   WGU[hc % 3][:, :], -128.0,
                   waits=[(ch["wgu_ld"].h, 16 * (pass_idx * NH + hc + 1))],
                   incs=[ch["wgu_c"].inc()])

    def emit_rope(PB, dst, toff, SCR1, SCR2, wait0, incs_last):
        tcc = bufs["Tcc"][:, toff:toff + 512]
        tss = bufs["Tss"][:, toff:toff + 512]
        P.emit("vector", "tensor_scalar_mul", SCR2[0:64, :], PB[64:128, :],
               -1.0, waits=list(wait0))
        P.emit("vector", "tensor_copy", SCR2[64:128, :], PB[0:64, :])
        P.emit("vector", "tensor_mul", SCR1[:, :], PB[:, :], tcc)
        P.emit("vector", "tensor_mul", SCR2[:, :], SCR2[:, :], tss)
        P.emit("vector", "tensor_add", dst, SCR1[:, :], SCR2[:, :],
               incs=list(incs_last))

    P.emit("sync", "dma_start", out=s_cst[:, :], in_=d_cst[:, :],
           incs=[ch["const"].inc(16)])
    P.emit("sync", "dma_start", out=s_crow[:, :], in_=d_crow[:, :],
           incs=[ch["const"].inc(16)])
    P.emit("gpsimd", "dma_start", out=s_cbf[:, 0:384], in_=d_cst[:, 0:384],
           incs=[ch["cbf"].inc(16)])
    P.emit("gpsimd", "dma_start", out=s_cbf[:, 384:385],
           in_=d_cst[:, C_ONE:C_ONE + 1], allow_slow_non_contiguous=True,
           incs=[ch["cbf"].inc(16)])
    P.emit("gpsimd", "dma_start", out=s_cbf[:, 392:520],
           in_=d_cst[:, 512:640], incs=[ch["cbf"].inc(16)])
    P.emit("vector", "tensor_scalar_mul", s_cst[:, C_SD:C_SD + NDO],
           s_cst[:, C_SD:C_SD + NDO], 0.5,
           waits=[ch["const"].at()], incs=[ch["const"].inc()])

    esA = ExitStack()
    X = esA.enter_context(nc.sbuf_tensor("X", [128, NK * TQ], F32))
    XN = esA.enter_context(nc.sbuf_tensor("XN", [128, NK * TQ], BF16))
    PROD = esA.enter_context(nc.sbuf_tensor("PROD", [128, HHALF * TQ], BF16))
    WGU = [esA.enter_context(nc.sbuf_tensor(f"WGU{i}", [128, 4096], BF16))
           for i in range(3)]
    WD = [esA.enter_context(nc.sbuf_tensor(f"WD{i}", [128, 2816], BF16))
          for i in range(2)]
    XSQ = [esA.enter_context(nc.sbuf_tensor(f"XSQ{i}", [128, TQ], F32))
           for i in range(2)]
    SG = [esA.enter_context(nc.sbuf_tensor(f"SGb{i}", [128, TQ], F32))
          for i in range(2)]
    psA = ExitStack()
    PP = [psA.enter_context(nc.psum_tensor(f"PP{i}", [128, 2048], F32))
          for i in range(2)]
    bufs.update(X=X, XN=XN, PROD=PROD)

    for k in range(NK):
        P.emit("sync", "dma_start",
               out=X[:, k * TQ:(k + 1) * TQ],
               in_=d_x.ap().rearrange("(k p) t -> p k t", p=128)[:, k, :],
               incs=[ch["xload"].inc(16)])

    ffn_prefetch(WGU, 0)
    emit_ln(X, XN, XSQ, SG, PP[0], PP[1],
            lambda k: [(ch["xload"].h, 16 * (k + 1))],
            k0_extra=[ch["const"].at()],
            bcast_extra=[ch["xload"].at()])
    tap("L1", [ch["xn"].at()])
    if stage == "L1":
        return _build([ch["xn"].at()])
    emit_ffn(X, XN, PROD, WGU, WD, SG, PP, 0, prologue_done=True)
    tap("F1", [ch["dn_ev"].at()])
    if stage == "F1":
        return _build([ch["dn_ev"].at()])
    emit_ln(X, XN, XSQ, SG, PP[0], PP[1],
            lambda k: [(ch["dn_ev"].h, 16 + k + 1)],
            k0_extra=[ch["prod"].at(), ch["silu"].at()],
            bcast_extra=[(ch["dn_ev"].h, 32)])
    tap("L2", [ch["xn"].at()])
    if stage == "L2":
        return _build([ch["xn"].at()])

    xn_b = ch["xn"].n - NK
    for half in range(2):
        P.emit("sync", "dma_start",
               out=d_x1in.ap().rearrange("(h k p) t -> p h k t", p=128, h=2)
               [:, half, :, :],
               in_=XN[:, half * 8 * TQ:(half + 1) * 8 * TQ]
               .rearrange("p (k t) -> p k t", k=8),
               waits=[(ch["xn"].h, xn_b + 8 * (half + 1))],
               incs=[ch["pk1"].inc(16)])
    P.emit("sync", "dma_start", out=d_xpark[:, :], in_=X[:, :],
           incs=[ch["park1"].inc(16)])
    P.emit("gpsimd", "collective_compute", "AllGather", ALU.bypass,
           replica_groups=[[0, 1], [2, 3], [4, 5], [6, 7]],
           ins=[d_x1in[0:1024, :].opt()], outs=[d_x1out_a.ap().opt()],
           waits=[(ch["pk1"].h, 16)], incs=[ch["cc1"].inc()])
    P.emit("gpsimd", "collective_compute", "AllGather", ALU.bypass,
           replica_groups=[[0, 1], [2, 3], [4, 5], [6, 7]],
           ins=[d_x1in[1024:2048, :].opt()], outs=[d_x1out_b.ap().opt()],
           waits=[(ch["pk1"].h, 32), (ch["cc1"].h, 1)],
           incs=[ch["cc1"].inc()])

    psA.close()
    esA.close()

    esB = ExitStack()
    K_f = esB.enter_context(nc.sbuf_tensor("K_f", [128, 2 * TG], BF16))
    VT_f = esB.enter_context(nc.sbuf_tensor("VT_f", [128, NQT * 2 * 128],
                                            BF16))
    Q = esB.enter_context(nc.sbuf_tensor("Q", [128, 8 * TG], BF16))
    esB2 = ExitStack()
    X2F = esB2.enter_context(nc.sbuf_tensor("X2F", [128, NK * TG], BF16))
    Tcc = esB2.enter_context(nc.sbuf_tensor("Tcc", [128, TG], F32))
    Tss = esB2.enter_context(nc.sbuf_tensor("Tss", [128, TG], F32))
    PW = esB2.enter_context(nc.sbuf_tensor("PW", [128, 4 * 2048], BF16))
    SCR1 = esB2.enter_context(nc.sbuf_tensor("SCR1", [128, 512], F32))
    SCR2 = esB2.enter_context(nc.sbuf_tensor("SCR2", [128, 512], F32))
    psB = ExitStack()
    PBA = psB.enter_context(nc.psum_tensor("PBA", [128, 512], F32))
    PBB = psB.enter_context(nc.psum_tensor("PBB", [128, 512], F32))
    PBC = psB.enter_context(nc.psum_tensor("PBC", [128, 256], F32))
    PBD = psB.enter_context(nc.psum_tensor("PBD", [128, 256], F32))
    bufs.update(K_f=K_f, VT_f=VT_f, Q=Q, X2F=X2F, Tcc=Tcc, Tss=Tss)

    sbr = [ch["park1"].at(), ch["pk1"].at()]
    P.emit("sync", "dma_start", out=Tcc[:, :], in_=d_tbl[:, 0:TG],
           waits=list(sbr), incs=[ch["tblld"].inc(16)])
    P.emit("sync", "dma_start", out=Tss[:, :], in_=d_tbl[:, TG:2 * TG],
           incs=[ch["tblld"].inc(16)])
    for half in range(2):
        srcH = (d_x1out_a, d_x1out_b)[half]
        for r in range(2):
            P.emit("sync", "dma_start",
                   out=X2F[:, half * 8 * TG:(half + 1) * 8 * TG]
                   .rearrange("p (k t) -> p k t", k=8)
                   [:, :, r * TQ:(r + 1) * TQ],
                   in_=srcH.ap().rearrange("(r k p) t -> p r k t",
                                           p=128, r=2)[:, r, :, :],
                   waits=[(ch["cc1"].h, half + 1)] if r == 0 else [],
                   incs=[ch["x2f"].inc(16)])
    for g in range(2):
        P.emit("gpsimd", "dma_start", out=PW[:, g * 2048:(g + 1) * 2048],
               in_=d_wav[g, :, :],
               waits=list(sbr) if g == 0 else [], incs=[ch["pw"].inc(16)])
    for g in range(2):
        P.emit("gpsimd", "dma_start",
               out=PW[:, (2 + g) * 2048:(3 + g) * 2048],
               in_=d_wak[g, :, :], incs=[ch["pw"].inc(16)])

    x2f_ok = ch["x2f"].at()
    x2f_b = ch["x2f"].n - 64
    pw_ok = ch["pw"].at()

    for g in range(2):
        for tcg in range(4):
            pb = (PBA, PBB)[tcg % 2]
            idx = g * 4 + tcg
            w = [pw_ok, ch["tblld"].at(), ch["const"].at()] \
                if idx == 0 else []
            if idx >= 2:
                w.append((ch["ropek"].h, idx - 1))
            for k in range(NK):
                wk = list(w) if k == 0 else []
                if idx == 0 and k in (0, 8):
                    wk.append((ch["x2f"].h, x2f_b + 32 * (k // 8 + 1)))
                P.emit("tensor", "matmul", pb[:, :],
                       PW[:, (2 + g) * 2048 + k * 128:
                          (2 + g) * 2048 + (k + 1) * 128],
                       X2F[:, k * TG + tcg * 512:k * TG + (tcg + 1) * 512],
                       start=(k == 0), stop=(k == NK - 1),
                       waits=wk,
                       incs=[ch["kmm"].inc()] if k == NK - 1 else [])
            emit_rope(pb, K_f[:, g * TG + tcg * 512:g * TG + (tcg + 1) * 512],
                      tcg * 512, SCR1, SCR2, [ch["kmm"].at()],
                      [ch["ropek"].inc()])

    for tt in range(NQT):
        pc = (PBC, PBD)[tt % 2]
        for mc in range(NK):
            w = []
            if tt == 0 and mc == 0:
                w = [pw_ok, (ch["x2f"].h, x2f_b + 32)]
            if tt == 0 and mc == 8:
                w = [(ch["x2f"].h, x2f_b + 64)]
            if mc == 0 and tt >= 2:
                w.append((ch["vev"].h, tt - 1))
            for g in range(2):
                P.emit("tensor", "matmul", pc[:, g * 128:(g + 1) * 128],
                       X2F[:, mc * TG + tt * 128:mc * TG + (tt + 1) * 128],
                       PW[:, g * 2048 + mc * 128:g * 2048 + (mc + 1) * 128],
                       start=(mc == 0 and g == 0),
                       stop=(mc == NK - 1 and g == 1),
                       waits=w if g == 0 else [],
                       incs=[ch["vmm"].inc()]
                       if (g == 1 and mc == NK - 1) else [])
        P.emit("scalar", "copy", VT_f[:, tt * 256:(tt + 1) * 256],
               pc[:, :], waits=[ch["vmm"].at()], incs=[ch["vev"].inc()])

    for h in range(8):
        slot = 2 + h % 2
        w = [(ch["kmm"].h, 4 * (slot - 1))] if h < 2 else \
            [(ch["qmm"].h, 4 * (h - 1))]
        P.emit("gpsimd", "dma_start",
               out=PW[:, slot * 2048:(slot + 1) * 2048],
               in_=d_waq[h, :, :], waits=w, incs=[ch["pw"].inc(16)])
        for qc in range(4):
            pb = (PBA, PBB)[qc % 2]
            idx = h * 4 + qc
            w = [(ch["pw"].h, 16 * (4 + h + 1))] if qc == 0 else []
            if idx == 0:
                w.append(ch["ropek"].at())
            if idx >= 2:
                w.append((ch["ropeq"].h, idx - 1))
            for k in range(NK):
                P.emit("tensor", "matmul", pb[:, :],
                       PW[:, slot * 2048 + k * 128:slot * 2048 + (k + 1) * 128],
                       X2F[:, k * TG + qc * 512:k * TG + (qc + 1) * 512],
                       start=(k == 0), stop=(k == NK - 1),
                       waits=w if k == 0 else [],
                       incs=[ch["qmm"].inc()] if k == NK - 1 else [])
            emit_rope(pb, Q[:, h * TG + qc * 512:h * TG + (qc + 1) * 512],
                      qc * 512, SCR1, SCR2, [ch["qmm"].at()],
                      [ch["ropeq"].inc()])

    tap("QKV", [ch["ropeq"].at(), ch["ropek"].at(), ch["vev"].at()])
    if stage == "QKV":
        return _build([ch["ropeq"].at(), ch["ropek"].at(), ch["vev"].at()])
    psB.close()
    esB2.close()

    esC = ExitStack()
    X2 = esC.enter_context(nc.sbuf_tensor("X2", [128, NK * TQ], F32))
    PT = [esC.enter_context(nc.sbuf_tensor(f"PT{i}", [128, 2048], BF16))
          for i in range(2)]
    SCRAP = esC.enter_context(nc.sbuf_tensor("SCRAP", [128, 1024], BF16))
    NMBH = esC.enter_context(nc.sbuf_tensor("NMBH", [1, 2048], BF16))
    NMBL = esC.enter_context(nc.sbuf_tensor("NMBL", [1, 2048], BF16))
    OTR = [esC.enter_context(nc.sbuf_tensor(f"OTR{i}", [128, 2048], BF16))
           for i in range(2)]
    LOHI = [esC.enter_context(nc.sbuf_tensor(f"LOHI{i}", [128, 8192], BF16))
            for i in range(2)]
    WAOB = [esC.enter_context(nc.sbuf_tensor(f"WAOB{i}", [128, 2048], BF16))
            for i in range(5)]
    psC = ExitStack()
    PS_A = psC.enter_context(nc.psum_tensor("PS_A", [128, 2048], F32))
    PS_B = psC.enter_context(nc.psum_tensor("PS_B", [128, 2048], F32))
    bufs.update(X2=X2, OTR0=OTR[0], OTR1=OTR[1], s_nmla=s_nmla, NMBH=NMBH,
                NMBL=NMBL, s_nm=s_nm, s_nm2=s_nm2, s_rs=s_rs, s_rs2=s_rs2,
                s_nmm=s_nmm, s_d1=s_d1, s_d2=s_d2, s_e1=s_e1, s_e2=s_e2,
                s_lnr=s_lnr, s_dbg=s_dbg)

    scr = [ch["ropeq"].at(), ch["vev"].at(), ch["qmm"].at(),
           ch["ropek"].at()]
    P.emit("sync", "dma_start", out=X2[:, :], in_=d_xpark[:, :],
           waits=list(scr), incs=[ch["xrel"].inc(16)])
    for s in range(4):
        P.emit("gpsimd", "dma_start", out=WAOB[s][:, :], in_=d_wao[s, :, :],
               waits=list(scr) if s == 0 else [],
               incs=[ch["waob"].inc(16)])

    ident_f = s_cst[:, C_ID:C_ID + 128]
    mask_fb = s_cbf[:, 128:256]
    mask_tb = s_cbf[:, 256:384]
    ident_bf = s_cbf[:, 0:128]
    onerow = s_crow[0:1, 128:256]

    onesbf = s_cbf[0:1, 392:520]
    NMROW = RA[0:1, 0:2048]

    def REG(i):
        return PS_A[:, (i % 2) * 1024:(i % 2 + 1) * 1024]

    segi = [0]
    prev_bias_done = [None]
    lnrow_base = {}
    nmb_ready = {}
    pv_after = {}

    def seg_wait():
        w = []
        if segi[0] >= 2:
            w.append((ch["seg"].h, segi[0] - 1))
        return w

    def REG(i):
        return PS_A[:, (i % 2) * 1024:(i % 2 + 1) * 1024]

    def emit_spass_qt(hh, qt):
        glh = hh // 4
        if qt == 0:
            lnrow_base[hh] = ch["lnrow"].n
        kw = (qt + 1) * 128
        nseg = 1 if kw <= 1024 else 2
        for sg in range(nseg):
            a = sg * 1024
            ws = min(kw, 1024) if sg == 0 else kw - 1024
            reg = REG(segi[0])
            w = seg_wait()
            if hh == 0 and qt == 0:
                w += list(scr) + [ch["cbf"].at()]
            last_seg = (sg == nseg - 1)
            nfull, rem = divmod(ws, 512)
            nch = nfull + (1 if rem else 0)
            for c in range(nch):
                n = 512 if c < nfull else rem
                last_ch = (c == nch - 1)
                P.emit("tensor", "matmul", reg[:, c * 512:c * 512 + n],
                       Q[:, hh * TG + qt * 128:hh * TG + (qt + 1) * 128],
                       K_f[:, glh * TG + a + c * 512:
                           glh * TG + a + c * 512 + n],
                       start=True,
                       stop=(not (last_seg and last_ch)),
                       waits=w if c == 0 else [],
                       incs=([ch["smm"].inc()]
                             if (last_ch and not last_seg) else []))
            if last_seg:
                m0 = ws - 128
                P.emit("tensor", "matmul", reg[:, m0:m0 + 128], ident_bf,
                       mask_fb, start=False, stop=True,
                       incs=[ch["smm"].inc()])
            nmdst = s_nm if sg == 0 else s_nm2
            rsdst = s_rs if sg == 0 else s_rs2
            P.emit("vector", "reduce_max", nmdst[:, :], reg[:, 0:ws],
                   axis=AX.X, negate=True,
                   waits=[ch["smm"].at(), ch["seg"].at()],
                   incs=[ch["negmax"].inc()])
            P.emit("scalar", "activation", SCRAP[:, 0:ws], reg[:, 0:ws],
                   AF.Exp, bias=nmdst[:, :], accum_out=rsdst[:, :],
                   waits=[ch["negmax"].at()], incs=[ch["seg"].inc()])
            if nseg == 2:
                P.emit("scalar", "copy", s_dbg[:, sg:sg + 1], rsdst[:, :])
            segi[0] += 1
        if nseg == 1:
            P.emit("scalar", "activation", s_lnr[:, :], s_rs[:, :],
                   AF.Ln, incs=[ch["mg2"].inc()])
            P.emit("vector", "tensor_sub", s_nmla[:, qt:qt + 1],
                   s_nm[:, :], s_lnr[:, :],
                   waits=[ch["mg2"].at()], incs=[ch["lnrow"].inc()])
        else:
            P.emit("vector", "tensor_sub", s_d1[:, :], s_nm[:, :],
                   s_nm2[:, :], waits=[ch["negmax"].at()])
            P.emit("vector", "tensor_sub", s_d2[:, :], s_nm2[:, :],
                   s_nm[:, :], incs=[ch["mg1"].inc()])
            P.emit("scalar", "activation", s_e1[:, :], s_d1[:, :],
                   AF.Relu, waits=[ch["mg1"].at()])
            P.emit("scalar", "activation", s_e2[:, :], s_d2[:, :],
                   AF.Relu)
            P.emit("scalar", "activation", s_d1[:, :], s_e1[:, :],
                   AF.Exp, scale=s_cst[:, C_NEG1:C_NEG1 + 1])
            P.emit("scalar", "activation", s_d2[:, :], s_e2[:, :],
                   AF.Exp, scale=s_cst[:, C_NEG1:C_NEG1 + 1])
            P.emit("scalar", "mul", s_rs[:, :], s_d1[:, :],
                   s_dbg[:, 0:1])
            P.emit("scalar", "mul", s_rs2[:, :], s_d2[:, :],
                   s_dbg[:, 1:2], incs=[ch["mg1"].inc()])
            P.emit("vector", "tensor_add", s_rs[:, :], s_rs[:, :],
                   s_rs2[:, :], waits=[ch["mg1"].at()],
                   incs=[ch["mg2"].inc()])
            P.emit("scalar", "activation", s_lnr[:, :], s_rs[:, :],
                   AF.Ln, waits=[ch["mg2"].at()])
            P.emit("scalar", "add", s_e2[:, :], s_lnr[:, :], s_e1[:, :],
                   incs=[ch["mg2"].inc()])
            P.emit("vector", "tensor_sub", s_nmla[:, qt:qt + 1],
                   s_nm[:, :], s_e2[:, :],
                   waits=[ch["mg2"].at()], incs=[ch["lnrow"].inc()])

    def emit_transposes_nmb(hh):
        for qt in range(NQT):
            wt = [(ch["lnrow"].h, lnrow_base[hh] + qt + 1)]
            if qt == 0:
                wt.append(ch["otev"].at())
            P.emit("tensor", "transpose",
                   PS_B[0:1, qt * 128:(qt + 1) * 128],
                   s_nmla[:, qt:qt + 1], ident_f, waits=wt,
                   incs=[ch["nmtr"].inc()] if qt == NQT - 1 else [])
        wb = [ch["nmtr"].at()]
        if prev_bias_done[0] is not None:
            wb.append(prev_bias_done[0])
        P.emit("vector", "tensor_copy", NMBH[0:1, :], PS_B[0:1, :], waits=wb)
        P.emit("vector", "tensor_copy", RA[0:1, 0:2048], NMBH[0:1, :])
        P.emit("vector", "tensor_sub", NMBL[0:1, :], PS_B[0:1, :],
               RA[0:1, 0:2048], incs=[ch["nmev"].inc(2)])
        nmb_ready[hh] = ch["nmev"].at()

    def emit_sT_j(hh, j):
        glh = hh // 4
        width = (NQT - j) * 128
        nseg = 2 if width > 1024 else 1
        for sg in range(nseg):
            qa = sg * 1024
            ws = (min(width, 1024) if sg == 0 else width - 1024)
            reg = REG(segi[0])
            w = seg_wait()
            if j == 0 and sg == 0:
                w.append(nmb_ready[hh])
            nfull, rem = divmod(ws, 512)
            nch = nfull + (1 if rem else 0)
            has_mask = (sg == 0)
            for c in range(nch):
                n = 512 if c < nfull else rem
                boff = j * 128 + qa + c * 512
                P.emit("tensor", "matmul", reg[:, c * 512:c * 512 + n],
                       onesbf, NMBH[0:1, boff:boff + n],
                       start=True, stop=False, waits=w if c == 0 else [])
                P.emit("tensor", "matmul", reg[:, c * 512:c * 512 + n],
                       onesbf, NMBL[0:1, boff:boff + n],
                       start=False, stop=False)
                P.emit("tensor", "matmul", reg[:, c * 512:c * 512 + n],
                       K_f[:, glh * TG + j * 128:glh * TG + (j + 1) * 128],
                       Q[:, hh * TG + boff:hh * TG + boff + n],
                       start=False, stop=(not (has_mask and c == 0)),
                       incs=([ch["smm"].inc()]
                             if (not has_mask and c == nch - 1) else []))
            if has_mask:
                P.emit("tensor", "matmul", reg[:, 0:128], ident_bf,
                       mask_tb, start=False, stop=True,
                       incs=[ch["smm"].inc()])
            we = [ch["smm"].at()]
            if j >= 2 and sg == 0:
                we.append(ch["pv"].at())
            P.emit("scalar", "activation", PT[j % 2][:, qa:qa + ws],
                   reg[:, 0:ws], AF.Exp,
                   waits=we, incs=[ch["seg"].inc()])
            segi[0] += 1
        pv_after[(hh, j)] = segi[0]

    def emit_pv(hh, j):
        glh = hh // 4
        nq = NQT - j
        wpv = [(ch["seg"].h, pv_after[(hh, j)])]
        for u in range(nq):
            qt = j + u
            P.emit("tensor", "matmul",
                   PS_B[:, qt * 128:(qt + 1) * 128],
                   VT_f[:, (j * 2 + glh) * 128:(j * 2 + glh + 1) * 128],
                   PT[j % 2][:, u * 128:(u + 1) * 128],
                   start=(j == 0 and qt % 4 == 0),
                   stop=(j == qt and qt % 4 == 3),
                   waits=wpv if u == 0 else [],
                   incs=[ch["pv"].inc()] if u == nq - 1 else [])

    s0 = s_cst[:, C_S0:C_S0 + 1]
    s1 = s_cst[:, C_S1:C_S1 + 1]
    order = [0, 2, 1, 3]

    def oprj_unpack(si):
        grp = order[si]
        w = [(ch["cc2"].h, 1 if grp in (0, 2) else 2)]
        if si >= 2:
            w.append((ch["opj"].h, 32 * (si - 1)))
        for i in range(4):
            hg = grp * 4 + i
            if grp in (0, 2):
                srcT = d_x2out_a
                row0 = (512 if grp == 2 else 0) + (hg % 8) * 128
            else:
                srcT = d_x2out_b
                row0 = (512 if grp == 3 else 0) + ((hg - 4) % 8) * 128
            for part in range(2):
                P.emit("sync", "dma_start",
                       out=LOHI[si % 2][:, i * 2048 + part * TQ:
                                        i * 2048 + (part + 1) * TQ],
                       in_=srcT[row0:row0 + 128,
                                part * TQ:(part + 1) * TQ],
                       waits=w if (i == 0 and part == 0) else [],
                       incs=[ch["lohi"].inc(16)])

    def oprj_blend(si):
        wbl = [(ch["lohi"].h, 16 * 8 * (si + 1))]
        if si >= 2:
            wbl.append((ch["opj"].h, 32 * (si - 1)))
        for i in range(4):
            L = LOHI[si % 2][:, i * 2048:i * 2048 + TQ]
            H = LOHI[si % 2][:, i * 2048 + TQ:(i + 1) * 2048]
            P.emit("vector", "tensor_scalar_mul", L, L, s0,
                   waits=wbl if i == 0 else [])
            P.emit("vector", "scalar_tensor_tensor", L, H, s1, L,
                   op0=ALU.mult, op1=ALU.add,
                   incs=[ch["blend"].inc()] if i == 3 else [])


    for qt in range(NQT):
        emit_spass_qt(0, qt)
    emit_transposes_nmb(0)
    for h in range(8):
        for j in range(NQT):
            emit_sT_j(h, j)
            if h < 7:
                emit_spass_qt(h + 1, j)
            if j >= 1:
                emit_pv(h, j - 1)
        emit_pv(h, NQT - 1)
        prev_bias_done[0] = (ch["smm"].h, ch["smm"].n)
        w = [ch["pv"].at()]
        if h >= 2:
            w.append((ch["otdma"].h, 16 * (h - 1)))
        P.emit("scalar", "copy", OTR[h % 2][:, :], PS_B[:, :],
               waits=w, incs=[ch["otev"].inc()])
        P.emit("sync", "dma_start", out=d_x2in[h * 128:(h + 1) * 128, :],
               in_=OTR[h % 2][:, :], waits=[ch["otev"].at()],
               incs=[ch["otdma"].inc(16)])
        if h == 3:
            P.emit("gpsimd", "collective_compute", "AllGather", ALU.bypass,
                   replica_groups=[[0, 1], [2, 3], [4, 5], [6, 7]],
                   ins=[d_x2in[0:512, :].opt()],
                   outs=[d_x2out_a.ap().opt()],
                   waits=[(ch["otdma"].h, 64)], incs=[ch["cc2"].inc()])
        if h < 7:
            emit_transposes_nmb(h + 1)
        if h == 6:
            oprj_unpack(0)
            oprj_blend(0)

    tap("ATTN", [ch["otdma"].at()])
    if stage == "ATTN":
        return _build([ch["otdma"].at()])
    P.emit("gpsimd", "collective_compute", "AllGather", ALU.bypass,
           replica_groups=[[0, 1], [2, 3], [4, 5], [6, 7]],
           ins=[d_x2in[512:1024, :].opt()],
           outs=[d_x2out_b.ap().opt()],
           waits=[ch["otdma"].at()], incs=[ch["cc2"].inc()])

    for si, grp in enumerate(order):
        if si < 3:
            for i in range(4):
                sg_strip = order[si + 1] * 4 + i
                s_pos = (si + 1) * 4 + i
                w2 = []
                if s_pos >= 5:
                    w2 = [(ch["opj"].h, 32 * ((s_pos - 5) // 4 + 1))]
                P.emit("gpsimd", "dma_start",
                       out=WAOB[s_pos % 5][:, :],
                       in_=d_wao[sg_strip, :, :], waits=w2,
                       incs=[ch["waob"].inc(16)])
            oprj_unpack(si + 1)
            oprj_blend(si + 1)
        for mc in range(NK):
            for tc in range(2):
                pp = (mc * 2 + tc) % 2
                pr = PS_A[:, pp * 512:(pp + 1) * 512]
                idx = si * 32 + mc * 2 + tc
                w3 = []
                if idx == 0:
                    w3 = [ch["expT"].at(), (ch["blend"].h, si + 1),
                          (ch["waob"].h, 16 * 4 * (si + 1)),
                          ch["xrel"].at()]
                elif mc == 0 and tc == 0:
                    w3 = [(ch["blend"].h, si + 1),
                          (ch["waob"].h, 16 * 4 * (si + 1))]
                if idx >= 2:
                    w3.append((ch["opj_ev"].h, idx - 1))
                for i in range(4):
                    P.emit("tensor", "matmul", pr,
                           WAOB[(si * 4 + i) % 5][:, mc * 128:(mc + 1) * 128],
                           LOHI[si % 2][:, i * 2048 + tc * 512:
                                        i * 2048 + tc * 512 + 512],
                           start=(i == 0), stop=(i == 3),
                           waits=w3 if i == 0 else [],
                           incs=[ch["opj"].inc()] if i == 3 else [])
                P.emit("vector", "tensor_add",
                       X2[:, mc * TQ + tc * 512:mc * TQ + (tc + 1) * 512],
                       X2[:, mc * TQ + tc * 512:mc * TQ + (tc + 1) * 512],
                       pr, waits=[ch["opj"].at()],
                       incs=[ch["opj_ev"].inc()])
    tap("OPRJ", [ch["opj_ev"].at()])
    if stage == "OPRJ":
        return _build([ch["opj_ev"].at()])
    for k in range(NK):
        P.emit("sync", "dma_start", out=d_xpark[:, k * TQ:(k + 1) * TQ],
               in_=X2[:, k * TQ:(k + 1) * TQ],
               waits=[(ch["opj_ev"].h, 96 + 2 * (k + 1))],
               incs=[ch["park2"].inc(16)])

    psC.close()
    esC.close()
    esB.close()

    esD = ExitStack()
    X3 = esD.enter_context(nc.sbuf_tensor("X3", [128, NK * TQ], F32))
    XN3 = esD.enter_context(nc.sbuf_tensor("XN3", [128, NK * TQ], BF16))
    PROD3 = esD.enter_context(nc.sbuf_tensor("PROD3", [128, HHALF * TQ],
                                             BF16))
    WGU3 = [esD.enter_context(nc.sbuf_tensor(f"WGU3{i}", [128, 4096], BF16))
            for i in range(3)]
    WD3 = [esD.enter_context(nc.sbuf_tensor(f"WD3{i}", [128, 2816], BF16))
           for i in range(2)]
    XSQ3 = [esD.enter_context(nc.sbuf_tensor(f"XSQ3{i}", [128, TQ], F32))
            for i in range(2)]
    SG3 = [esD.enter_context(nc.sbuf_tensor(f"SG3{i}", [128, TQ], F32))
           for i in range(2)]
    psD = ExitStack()
    PP3 = [psD.enter_context(nc.psum_tensor(f"PP3{i}", [128, 2048], F32))
           for i in range(2)]
    bufs.update(X3=X3, XN3=XN3)

    for k in range(NK):
        P.emit("sync", "dma_start", out=X3[:, k * TQ:(k + 1) * TQ],
               in_=d_xpark[:, k * TQ:(k + 1) * TQ],
               waits=[(ch["park2"].h, 16 * (k + 1))],
               incs=[ch["xrel2"].inc(16)])
    ffn_prefetch(WGU3, 1, dma_first_waits=[ch["opj"].at(),
                                            ch["park2"].at(),
                                            ch["otdma"].at()])
    emit_ln(X3, XN3, XSQ3, SG3, PP3[0], PP3[1],
            lambda k: [(ch["xrel2"].h, 16 * (k + 1))],
            k0_extra=[ch["otev"].at()],
            bcast_extra=[ch["opj_ev"].at()])
    tap("L3", [ch["xn"].at()])
    if stage == "L3":
        return _build([ch["xn"].at()])
    emit_ffn(X3, XN3, PROD3, WGU3, WD3, SG3, PP3, 1,
             pp1_waits=[ch["otev"].at()],
             dma_first_waits=[ch["opj"].at(), ch["park2"].at(),
                              ch["otdma"].at()], prologue_done=True)
    tap("F2", [ch["dn_ev"].at()])

    for k in range(NK):
        P.emit("sync", "dma_start",
               out=d_y[k * 128:(k + 1) * 128, :],
               in_=X3[:, k * TQ:(k + 1) * TQ],
               waits=[(ch["dn_ev"].h, 48 + k + 1)],
               incs=[ch["out"].inc(16)])
    psD.close()
    esD.close()
    return _build([ch["out"].at()])


def _prep_inputs(L0_attn_x, L0_attn_w_aq, L0_attn_w_ak, L0_attn_w_av,
                 L0_attn_w_ao, L0_ffn_wq_gate, L0_ffn_s_gate, L0_ffn_wq_up,
                 L0_ffn_s_up, L0_ffn_wq_down, L0_ffn_s_down):
    x = np.asarray(L0_attn_x, dtype=np.float32)
    wq_gate = np.asarray(L0_ffn_wq_gate, dtype=np.int32)
    wq_up = np.asarray(L0_ffn_wq_up, dtype=np.int32)
    wq_down = np.asarray(L0_ffn_wq_down, dtype=np.int32)
    w_aq = np.asarray(L0_attn_w_aq, dtype=np.float32)
    w_ak = np.asarray(L0_attn_w_ak, dtype=np.float32)
    w_av = np.asarray(L0_attn_w_av, dtype=np.float32)
    w_ao = np.asarray(L0_attn_w_ao, dtype=np.float32)
    s_g = np.asarray(L0_ffn_s_gate, dtype=np.float32)
    s_u = np.asarray(L0_ffn_s_up, dtype=np.float32)
    s_d = np.asarray(L0_ffn_s_down, dtype=np.float32)

    g_t = wq_gate.reshape(NH, 128, NK, 128).transpose(0, 3, 2, 1) \
        .reshape(NH, 128, 2048)
    u_t = wq_up.reshape(NH, 128, NK, 128).transpose(0, 3, 2, 1) \
        .reshape(NH, 128, 2048)
    wgu_t = np.ascontiguousarray(np.concatenate([g_t, u_t], axis=2))
    wd_t = np.ascontiguousarray(
        wq_down.reshape(NDO, 128, NH, 128).transpose(0, 3, 2, 1)
        .reshape(NDO, 128, 5632))
    waq_full = np.ascontiguousarray(
        w_aq.reshape(4, 4, NK, 128, 128).transpose(0, 1, 3, 2, 4)
        .reshape(16, 128, 2048))
    wak_full = np.ascontiguousarray(
        w_ak.reshape(4, NK, 128, 128).transpose(0, 2, 1, 3)
        .reshape(4, 128, 2048))
    wav_full = np.ascontiguousarray(
        w_av.reshape(4, NK, 128, 128).transpose(0, 2, 1, 3)
        .reshape(4, 128, 2048))
    wao_t = np.ascontiguousarray(w_ao.reshape(16, 128, 2048))

    mult = np.float32(128.0 ** -0.25)
    pos = np.arange(TG, dtype=np.float32)
    freqs = np.power(np.float32(10000.0),
                     -(np.arange(64, dtype=np.float32) / np.float32(64)))
    ang = freqs[:, None] * pos[None, :]
    cos = np.cos(ang).astype(np.float32) * mult
    sin = np.sin(ang).astype(np.float32) * mult
    tbl = np.ascontiguousarray(
        np.concatenate([np.concatenate([cos, cos], 0),
                        np.concatenate([sin, sin], 0)], 1)).astype(np.float32)

    ident = np.eye(128, dtype=np.float32)
    tri = np.triu(np.ones((128, 128), dtype=np.float32), 1)
    mask_f = tri * np.float32(-1e30)
    mask_t = np.ascontiguousarray(tri.T) * np.float32(-1e30)

    cstrow = np.zeros((1, 256), dtype=np.float32)
    cstrow[0, 0:128] = -128.0
    cstrow[0, 128:256] = 1.0

    in_maps = []
    for c in range(8):
        b, par = c // 2, c % 2
        rows = slice(1024 * par, 1024 * par + 1024)
        x_fm = np.ascontiguousarray(x[b, rows, :].T)
        cst = np.zeros((128, 640), dtype=np.float32)
        cst[:, 512:640] = 1.0
        cst[:, 0:128] = ident
        cst[:, 128:256] = mask_f
        cst[:, 256:384] = mask_t
        cst[:, 384:428] = s_g.reshape(NH, 128).T
        cst[:, 428:472] = s_u.reshape(NH, 128).T
        cst[:, 472:488] = s_d.reshape(NDO, 128).T
        cst[:, 488] = 1.0
        cst[:, 489] = 1.0 - par
        cst[:, 490] = float(par)
        cst[:, 491] = EPS
        cst[:, 492] = -0.5
        cst[:, 493] = -1.0
        in_maps.append(dict(
            x_fm=x_fm, wgu_t=wgu_t, wd_t=wd_t,
            waq_t=np.ascontiguousarray(waq_full[8 * par:8 * par + 8]),
            wak_t=np.ascontiguousarray(wak_full[2 * par:2 * par + 2]),
            wav_t=np.ascontiguousarray(wav_full[2 * par:2 * par + 2]),
            wao_t=wao_t, cst=cst, cstrow=cstrow, tbl=tbl))
    return in_maps


def kernel(**inputs):
    in_maps = _prep_inputs(**inputs)
    nc = build_nc(taps=_TAPS)
    res = run_bass_kernel_spmd(nc, in_maps, core_ids=list(range(8)))
    out = np.empty((4, 2048, 2048), dtype=np.float32)
    for c in range(8):
        b, par = c // 2, c % 2
        rows = slice(1024 * par, 1024 * par + 1024)
        out[b, rows, :] = res.results[c]["y_fm"].T
    kernel._last_results = res
    return out
